# version 01
# baseline (speedup 1.0000x reference)
import numpy as np
import ml_dtypes
from contextlib import ExitStack
import concourse.bass as bass
import concourse.mybir as mybir
from concourse.bass_utils import run_bass_kernel_spmd
from concourse.bass_core_rust import IndirectOffsetOnAxis

F32 = mybir.dt.float32
BF16 = mybir.dt.bfloat16
U32 = mybir.dt.uint32
AF = mybir.ActivationFunctionType
ALU = mybir.AluOpType
AX = mybir.AxisListType

ENGINES = ("te", "ve", "act", "po", "sp")

class Prog:
    def __init__(self, name="prog"):
        self.nc = bass.Bass("TRN2", target_bir_lowering=False)
        self.es = ExitStack()
        self.ops = {e: [] for e in ENGINES}
        self.sems = {}
        self.counts = {}
        self.name = name

    def sem(self, name):
        if name not in self.sems:
            self.sems[name] = self.es.enter_context(self.nc.semaphore(name))
            self.counts[name] = 0
        return self.sems[name]

    def at(self, name):
        self.sem(name)
        return self.counts[name]

    def sb(self, name, shape, dtype):
        es = getattr(self, "_scope_es", None) or self.es
        return es.enter_context(self.nc.sbuf_tensor(name, shape, dtype))

    def scope(self):
        from contextlib import ExitStack, contextmanager
        @contextmanager
        def _s():
            old = getattr(self, "_scope_es", None)
            self._scope_es = ExitStack()
            try:
                yield
            finally:
                self._scope_es.close()
                self._scope_es = old
        return _s()

    def ps(self, name, shape, dtype=F32):
        return self.es.enter_context(self.nc.psum_tensor(name, shape, dtype))

    def din(self, name, shape, dtype):
        return self.nc.dram_tensor(name, shape, dtype, kind="ExternalInput").ap()

    def dout(self, name, shape, dtype):
        return self.nc.dram_tensor(name, shape, dtype, kind="ExternalOutput").ap()

    def dtmp(self, name, shape, dtype):
        return self.nc.dram_tensor(name, shape, dtype).ap()

    AUTOSYNC = ("ve", "act", "po")

    def emit(self, eng, fn, waits=(), inc=None, inc_k=1, tag="", chain=True):
        w = [(s, v) for (s, v) in waits]
        if eng in self.AUTOSYNC and chain:
            ec = eng + "_c"
            if inc is None:
                inc = ec
                inc_k = 1
            if inc == ec:
                w.append((ec, self.at(ec)))
        self.ops[eng].append((fn, w, inc, inc_k, tag))
        if inc is not None:
            self.sem(inc)
            self.counts[inc] += inc_k
            return self.counts[inc]
        return None

    def dma(self, eng, out, in_, waits=(), stream=None, tag="", **kw):
        s = stream or (eng + "_dma")
        return self.emit(eng, lambda E, o=out, i=in_: E.dma_start(out=o, in_=i, **kw),
                         waits=waits, inc=s, inc_k=16, tag=tag)

    def finalize(self, final_waits=()):
        for (s, v) in final_waits:
            self.emit("sp", lambda E: None, waits=[(s, v)], tag="final_wait")
        nc = self.nc
        with nc.Block() as blk:
            def mk_body(ename):
                def body(E):
                    for fn, waits, inc, inc_k, tag in self.ops[ename]:
                        for (s, v) in waits:
                            E.wait_ge(self.sems[s], v)
                        instr = fn(E)
                        if inc is not None:
                            assert instr is not None, f"op with inc returned None: {tag}"
                            instr.then_inc(self.sems[inc], inc_k)
                return body
            if self.ops["sp"]:
                blk.sync(mk_body("sp"))
            if self.ops["te"]:
                blk.tensor(mk_body("te"))
            if self.ops["ve"]:
                blk.vector(mk_body("ve"))
            if self.ops["act"]:
                blk.scalar(mk_body("act"))
            if self.ops["po"]:
                blk.gpsimd(mk_body("po"))
        return nc

    def close(self):
        self.es.close()


def t_matmul(p, out_ap, lhsT, rhs, start, stop, waits=(), inc=None, tag=""):
    return p.emit("te", lambda E, o=out_ap, l=lhsT, r=rhs, st=start, sp=stop:
                  E.matmul(o, l, r, start=st, stop=sp),
                  waits=waits, inc=inc, tag=tag)

def t_transpose(p, out_ap, in_ap, ident, waits=(), inc=None, tag=""):
    return p.emit("te", lambda E, o=out_ap, i=in_ap, d=ident: E.transpose(o, i, d),
                  waits=waits, inc=inc, tag=tag)

def v_op(p, fn, waits=(), inc=None, tag=""):
    return p.emit("ve", fn, waits=waits, inc=inc, tag=tag)

def a_op(p, fn, waits=(), inc=None, tag=""):
    return p.emit("act", fn, waits=waits, inc=inc, tag=tag)

def g_op(p, fn, waits=(), inc=None, inc_k=1, tag=""):
    return p.emit("po", fn, waits=waits, inc=inc, inc_k=inc_k, tag=tag)

def bcast_ap(t, P, n, m, col=0):
    ap = t[:, col:col+n]
    a = ap.ap
    return bass.AP(ap.tensor, ap.offset, [list(a[0]), list(a[1]), [0, m]])


"""Shared MoE routing emitter.

Computes, replicated from full-sequence logits [8192,16] fp32 (DRAM, token-major):
  - per tile j (64 tiles of 128 tokens): softmax probs, top-2 (w1,w2,e1,e2)
  - global exclusive cumsum positions pos1/pos2 per token
Results left in SBUF arrays (fp32):
  warr1/warr2 [128,64], earr1/earr2 [128,64], pos1/pos2 [128,64]
Caller decides what to do with them (scatter gather-lists / compute slotids).
"""

def emit_routing(p, lg_dram, own_tiles=None):
    if own_tiles is None:
        own_tiles = list(range(64))
    lgt = p.sb("r_lgt", [128, 64 * 16], F32)
    exs = p.sb("r_exs", [128, 16], F32)
    m1 = p.sb("r_m1", [128, 1], F32)
    nm = p.sb("r_nm", [128, 1], F32)
    ssum = p.sb("r_ssum", [128, 1], F32)
    rs = p.sb("r_rs", [128, 1], F32)
    probs = p.sb("r_probs", [128, 16], F32)
    m8 = p.sb("r_m8", [128, 8], F32)
    i8 = p.sb("r_i8", [128, 8], U32)
    i1f = p.sb("r_i1f", [128, 1], F32)
    i2f = p.sb("r_i2f", [128, 1], F32)
    oh1 = p.sb("r_oh1", [128, 64 * 16], BF16)
    oh2 = p.sb("r_oh2", [128, 64 * 16], BF16)
    ohs = p.sb("r_ohs", [128, 16], F32)
    iota16 = p.sb("r_iota16", [128, 16], F32)
    nem = p.sb("r_nem", [16, 8192], BF16)
    ones16 = p.sb("r_ones16", [16, 1], F32)
    sincl = p.sb("r_sincl", [16, 8192], F32)
    sext = p.sb("r_sext", [128, 16], F32)
    tmp16 = p.sb("r_tmp16", [128, 16], F32)
    warr1 = p.sb("r_w1", [128, 64], F32)
    warr2 = p.sb("r_w2", [128, 64], F32)
    earr1 = p.sb("r_e1", [128, 64], F32)
    earr2 = p.sb("r_e2", [128, 64], F32)
    pos1 = p.sb("r_pos1", [128, 64], F32)
    pos2 = p.sb("r_pos2", [128, 64], F32)
    trp16 = p.ps("r_trp16", [16, 128], F32)
    trp128 = p.ps("r_trp128", [128, 16], F32)

    p.dma("sp", lgt[:], bass.AP(lg_dram.tensor, 0, [[16, 128], [2048, 64], [1, 16]]),
          stream="r_ld", tag="lgt")
    lg_done = p.at("r_ld")

    g_op(p, lambda E: E.iota(iota16[:], [[1, 16]], base=0, channel_multiplier=0,
                             allow_small_or_imprecise_dtypes=True), tag="iota16")
    io_done = p.at("po_c")
    v_op(p, lambda E: E.memset(ones16[:], 1.0), tag="ones16")

    id_done = p.at("ve_c")
    id_f = p.id_f

    for j in range(64):
        lj = lgt[:, j * 16:(j + 1) * 16]
        v_op(p, lambda E, lj=lj: E.reduce_max(m1[:], lj, axis=AX.X),
             waits=[("r_ld", lg_done)], tag=f"rm{j}")
        v_op(p, lambda E: E.tensor_scalar(nm[:], m1[:], -1.0, None, ALU.mult), tag=f"rn{j}")
        nm_ve = p.at("ve_c")
        ae = a_op(p, lambda E, lj=lj: E.activation(exs[:], lj, AF.Exp, bias=nm[:], scale=1.0),
                  waits=[("ve_c", nm_ve)], tag=f"re{j}")
        v_op(p, lambda E: E.reduce_sum(ssum[:], exs[:], axis=AX.X),
             waits=[("act_c", ae)], tag=f"rs{j}")
        v_op(p, lambda E: E.reciprocal(rs[:], ssum[:]), tag=f"rr{j}")
        v_op(p, lambda E: E.tensor_scalar(probs[:], exs[:], rs[:], None, ALU.mult), tag=f"rp{j}")
        v_op(p, lambda E: E.max(m8[:], probs[:]), tag=f"rx{j}")
        v_op(p, lambda E: E.max_index(i8[:], m8[:], probs[:]), tag=f"ri{j}")
        v_op(p, lambda E, j=j: E.tensor_copy(warr1[:, j:j + 1], m8[:, 0:1]), tag=f"rw1{j}")
        v_op(p, lambda E, j=j: E.tensor_copy(warr2[:, j:j + 1], m8[:, 1:2]), tag=f"rw2{j}")
        v_op(p, lambda E: E.tensor_copy(i1f[:], i8[:, 0:1]), tag=f"rc1{j}")
        v_op(p, lambda E: E.tensor_copy(i2f[:], i8[:, 1:2]), tag=f"rc2{j}")
        v_op(p, lambda E, j=j: E.tensor_copy(earr1[:, j:j + 1], i1f[:]), tag=f"re1{j}")
        v_op(p, lambda E, j=j: E.tensor_copy(earr2[:, j:j + 1], i2f[:]), tag=f"re2{j}")
        v_op(p, lambda E, j=j: E.tensor_scalar(oh1[:, j * 16:(j + 1) * 16], iota16[:],
                                               i1f[:], None, ALU.is_equal),
             waits=[("po_c", io_done)], tag=f"ro1{j}")
        v_op(p, lambda E, j=j: E.tensor_scalar(oh2[:, j * 16:(j + 1) * 16], iota16[:],
                                               i2f[:], None, ALU.is_equal), tag=f"ro2{j}")
        v_op(p, lambda E, j=j: E.tensor_add(ohs[:], oh1[:, j * 16:(j + 1) * 16],
                                            oh2[:, j * 16:(j + 1) * 16]), tag=f"roa{j}")
        ve_ohs = p.at("ve_c")
        tt = t_transpose(p, trp16[:], ohs[:], id_f[:],
                         waits=[("ve_c", ve_ohs)], inc="te_c", tag=f"rt{j}")
        v_op(p, lambda E, j=j: E.tensor_copy(nem[:, j * 128:(j + 1) * 128], trp16[:]),
             waits=[("te_c", tt)], tag=f"rte{j}")
    nem_done = p.at("ve_c")
    ones_b = bass.AP(ones16[:].tensor, 0, [list(ones16[:].ap[0]), [0, 8192]])
    v_op(p, lambda E: E.tensor_tensor_scan(sincl[:], ones_b, nem[:], 0.0,
                                           ALU.mult, ALU.add),
         waits=[("ve_c", nem_done)], tag="rscan")
    v_op(p, lambda E: E.tensor_sub(sincl[:], sincl[:], nem[:]), tag="rsub")
    sx_done = p.at("ve_c")
    for j in own_tiles:
        tt = t_transpose(p, trp128[:], sincl[:, j * 128:(j + 1) * 128], id_f[0:16, 0:16],
                         waits=[("ve_c", sx_done), ("ve_c", p.at("ve_c"))],
                         inc="te_c", tag=f"rpt{j}")
        ve0 = v_op(p, lambda E: E.tensor_copy(sext[:], trp128[:]),
                   waits=[("te_c", tt)], tag=f"rpe{j}")
        v_op(p, lambda E, j=j: E.tensor_mul(tmp16[:], oh1[:, j * 16:(j + 1) * 16], sext[:]),
             tag=f"rq1{j}")
        v_op(p, lambda E, j=j: E.reduce_sum(pos1[:, j:j + 1], tmp16[:], axis=AX.X),
             tag=f"rr1{j}")
        v_op(p, lambda E, j=j: E.tensor_mul(tmp16[:], oh2[:, j * 16:(j + 1) * 16], sext[:]),
             tag=f"rq2{j}")
        v_op(p, lambda E, j=j: E.reduce_sum(pos2[:, j:j + 1], tmp16[:], axis=AX.X),
             tag=f"rr2{j}")
    done = p.at("ve_c")
    return dict(warr1=warr1, warr2=warr2, earr1=earr1, earr2=earr2,
                pos1=pos1, pos2=pos2, done=done)

"""Launch A: LayerNorm(x) + gate logits + q-hat/k-hat/v projections (layer 0).
Per-core: tokens = its 8 strided chunks (1024 tokens).
"""

def build_A():
    p = Prog("A")
    x = p.din("x", [1024, 2048], F32)
    wq = p.din("wq", [2048, 2048], F32)
    wk = p.din("wk", [2048, 512], F32)
    wv = p.din("wv", [2048, 512], F32)
    gwt = p.din("gwt", [2048, 16], F32)
    xn_tm = p.dout("xn_tm", [1024, 2048], BF16)
    xn_fm = p.dout("xn_fm", [2048, 1024], BF16)
    logitsT = p.dout("logitsT", [16, 1024], F32)
    qh_fm = p.dout("qh_fm", [2048, 1024], BF16)
    kh_fm = p.dout("kh_fm", [512, 1024], BF16)
    v_tm = p.dout("v_tm", [1024, 512], BF16)

    x_sb = p.sb("x_sb", [128, 8 * 2048], F32)
    xnf32 = p.sb("xnf32", [128, 2048], F32)
    xn_tm_bf = p.sb("xn_tm_bf", [128, 2048], BF16)
    xn_fm_sb = p.sb("xn_fm_sb", [128, 16 * 1024], BF16)
    lgscr = p.sb("lgscr", [128, 2048], F32)
    gwt_sb = p.sb("gwt_sb", [128, 16 * 16], F32)
    lgT_sb = p.sb("lgT_sb", [16, 1024], F32)
    sumx = p.sb("sumx", [128, 8], F32)
    ssq = p.sb("ssq", [128, 8], F32)
    meanv = p.sb("meanv", [128, 8], F32)
    varv = p.sb("varv", [128, 8], F32)
    m2 = p.sb("m2", [128, 8], F32)
    rstd = p.sb("rstd", [128, 8], F32)
    sq_scr = p.sb("sq_scr", [128, 2048], F32)
    idw = p.sb("idw", [128, 128], F32)
    id_f = p.sb("id_f", [128, 128], F32)
    id_b = p.sb("id_b", [128, 128], BF16)
    wstage = p.sb("wstage", [128, 512], F32)
    wpan = p.sb("wpan", [128, 16 * 512], BF16)
    sqq = p.sb("sqq", [128, 512], F32)
    ssqq = p.sb("ssqq", [128, 4], F32)
    rstd4 = p.sb("rstd4", [128, 4], F32)
    qtm_sb = p.sb("qtm_sb", [128, 512], BF16)
    qev = p.sb("qev", [128, 128], BF16)
    vtm_sb = p.sb("vtm_sb", [128, 512], BF16)

    trps = p.ps("trps", [128, 128], F32)
    lgps = p.ps("lgps", [16, 128], F32)
    qps = p.ps("qps", [128, 512], F32)
    trbf = p.ps("trbf", [128, 128], BF16)

    for t in range(8):
        p.dma("sp", x_sb[:, t * 2048:(t + 1) * 2048], x[t * 128:(t + 1) * 128, :],
              stream="ld", tag=f"x{t}")
    ld_x = p.at("ld")
    p.dma("sp", gwt_sb[:], bass.AP(gwt.tensor, 0, [[16, 128], [2048, 16], [1, 16]]),
          stream="ld", tag="gwt")
    ld_gwt = p.at("ld")
    eps5 = p.sb("eps5", [128, 1], F32)
    eps6 = p.sb("eps6", [128, 1], F32)
    v_op(p, lambda E: E.memset(eps5[:], 1e-5), inc="ve_c", tag="eps5")
    v_op(p, lambda E: E.memset(eps6[:], 1e-6), inc="ve_c", tag="eps6")
    eps_done = p.at("ve_c")
    g_op(p, lambda E: E.iota(idw[:], [[-1, 128]], base=0, channel_multiplier=1,
                             allow_small_or_imprecise_dtypes=True), inc="po_c", tag="iota")
    iota_done = p.at("po_c")
    idf_done = v_op(p, lambda E: E.tensor_scalar(id_f[:], idw[:], 0.0, None, ALU.is_equal),
         waits=[("po_c", iota_done)], inc="ve_c", tag="idf")
    v_op(p, lambda E: E.tensor_copy(id_b[:], id_f[:]), waits=[("ve_c", idf_done)], inc="ve_c", tag="idb")
    id_done = p.at("ve_c")

    prev_tile_te = 0
    prev_mm_te = 0
    for t in range(8):
        xt = x_sb[:, t * 2048:(t + 1) * 2048]
        v_op(p, lambda E, xt=xt, t=t: E.reduce_sum(sumx[:, t:t + 1], xt, axis=AX.X),
             waits=[("ld", p.at("ld"))], inc="ve_c", tag=f"sum{t}")
        a_op(p, lambda E, xt=xt, t=t: E.activation(sq_scr[:], xt, AF.Square,
                                                   accum_out=ssq[:, t:t + 1]),
             waits=[("ld", p.at("ld"))], inc="act_c", tag=f"ssq{t}")
        act_ssq = p.at("act_c")
        v_op(p, lambda E, t=t: E.tensor_scalar(meanv[:, t:t + 1], sumx[:, t:t + 1],
                                               1.0 / 2048, None, ALU.mult),
             inc="ve_c", tag=f"mean{t}")
        v_op(p, lambda E, t=t: E.tensor_scalar(varv[:, t:t + 1], ssq[:, t:t + 1],
                                               1.0 / 2048, None, ALU.mult),
             waits=[("act_c", act_ssq)], inc="ve_c", tag=f"var{t}")
        v_op(p, lambda E, t=t: E.scalar_tensor_tensor(m2[:, t:t + 1], meanv[:, t:t + 1],
                                                      1.0, meanv[:, t:t + 1],
                                                      ALU.mult, ALU.mult),
             inc="ve_c", tag=f"m2_{t}")
        v_op(p, lambda E, t=t: E.tensor_sub(varv[:, t:t + 1], varv[:, t:t + 1],
                                            m2[:, t:t + 1]), inc="ve_c", tag=f"v2_{t}")
        ve_var = p.at("ve_c")
        a_op(p, lambda E, t=t: E.activation(rstd[:, t:t + 1], varv[:, t:t + 1], AF.Ln,
                                            bias=eps5[:], scale=1.0),
             waits=[("ve_c", max(ve_var, eps_done))], inc="act_c", tag=f"ln{t}")
        a_op(p, lambda E, t=t: E.activation(rstd[:, t:t + 1], rstd[:, t:t + 1], AF.Exp,
                                            bias=0.0, scale=-0.5),
             inc="act_c", tag=f"exp{t}")
        act_rstd = p.at("act_c")
        v_op(p, lambda E, xt=xt, t=t: E.tensor_scalar(xnf32[:], xt, meanv[:, t:t + 1],
                                                      rstd[:, t:t + 1], ALU.subtract,
                                                      ALU.mult),
             waits=[("act_c", act_rstd), ("te_c", prev_tile_te)],
             inc="ve_c", tag=f"xn{t}")
        ve_xn = p.at("ve_c")
        v_op(p, lambda E: E.tensor_copy(xn_tm_bf[:], xnf32[:]), inc="ve_c",
             waits=[("sp_st", max(0, p.counts.get("sp_st", 0)))], tag=f"xntm{t}")
        ve_xntm = p.at("ve_c")
        p.dma("sp", xn_tm[t * 128:(t + 1) * 128, :], xn_tm_bf[:],
              waits=[("ve_c", ve_xntm)], stream="sp_st", tag=f"oxntm{t}")
        for c in range(16):
            te_tr = t_transpose(p, trps[:], xnf32[:, c * 128:(c + 1) * 128], id_f[:],
                                waits=[("ve_c", ve_xn), ("ve_c", id_done),
                                       ("act_c", p.at("act_c")), ("ve_c", p.at("ve_c"))],
                                inc="te_c", tag=f"tr{t}_{c}")
            a_ev = a_op(p, lambda E, c=c: E.activation(lgscr[:, c * 128:(c + 1) * 128],
                                                       trps[:], AF.Identity),
                        waits=[("te_c", te_tr)], inc="act_c", tag=f"lgev{t}_{c}")
            v_op(p, lambda E, c=c, t=t: E.tensor_copy(
                xn_fm_sb[:, c * 1024 + t * 128: c * 1024 + (t + 1) * 128],
                lgscr[:, c * 128:(c + 1) * 128]),
                waits=[("act_c", a_ev)], inc="ve_c", tag=f"fmev{t}_{c}")
            te_mm = t_matmul(p, lgps[:, 0:128], gwt_sb[:, c * 16:(c + 1) * 16],
                             lgscr[:, c * 128:(c + 1) * 128],
                             start=(c == 0), stop=(c == 15),
                             waits=[("act_c", a_ev), ("ld", ld_gwt),
                                    ("ve_c", p.at("ve_c"))],
                             inc="te_c", tag=f"lgmm{t}_{c}")
        prev_mm_te = p.at("te_c")
        v_op(p, lambda E, t=t: E.tensor_copy(lgT_sb[:, t * 128:(t + 1) * 128],
                                             lgps[:, 0:128]),
             waits=[("te_c", prev_mm_te)], inc="ve_c", tag=f"lgevac{t}")
        prev_tile_te = prev_mm_te
    ve_lg = p.at("ve_c")
    p.dma("sp", logitsT, lgT_sb[:], waits=[("ve_c", ve_lg)], stream="sp_st", tag="olgT")
    ve_fm_all = p.at("ve_c")
    p.dma("sp", bass.AP(xn_fm.tensor, 0, [[1024, 128], [128 * 1024, 16], [1, 1024]]),
          xn_fm_sb[:], waits=[("ve_c", ve_fm_all)], stream="sp_st", tag="oxnfm")

    def proj(wdram, nheads_grp, out_fm_dram, norm_eps, wtag):
        for grp in range(nheads_grp):
            conv_done = 0
            for c in range(16):
                ldv = p.dma("sp", wstage[:],
                            wdram[c * 128:(c + 1) * 128, grp * 512:(grp + 1) * 512],
                            stream=f"ld_{wtag}", tag=f"{wtag}ld{grp}_{c}")
                cv = v_op(p, lambda E, c=c: E.tensor_copy(
                    wpan[:, c * 512:(c + 1) * 512], wstage[:]),
                    waits=[(f"ld_{wtag}", ldv), ("te_c", p.at("te_c"))],
                    inc="ve_c", tag=f"{wtag}cv{grp}_{c}")
                conv_done = cv
                p.ops["sp"].append((lambda E: None, [("ve_c", cv)], None, 1, "wsync"))
            prev_cv = 0
            for t in range(8):
                for c in range(16):
                    t_matmul(p, qps[:, 0:512],
                             xn_fm_sb[:, c * 1024 + t * 128: c * 1024 + (t + 1) * 128],
                             wpan[:, c * 512:(c + 1) * 512],
                             start=(c == 0), stop=(c == 15),
                             waits=[("ve_c", max(conv_done, prev_cv))] if c == 0 else (),
                             inc="te_c", tag=f"{wtag}mm{grp}_{t}_{c}")
                te_q = p.at("te_c")
                if norm_eps is None:
                    cv = v_op(p, lambda E: E.tensor_copy(qtm_sb[:], qps[:, 0:512]),
                              waits=[("te_c", te_q)], inc="ve_c", tag=f"{wtag}ev{grp}_{t}")
                else:
                    a1 = a_op(p, lambda E: E.activation(sqq[:], qps[:, 0:512], AF.Square),
                              waits=[("te_c", te_q)], inc="act_c", tag=f"{wtag}sq{grp}_{t}")
                    r1 = v_op(p, lambda E: E.tensor_reduce(
                        ssqq[:], sqq[:].rearrange("p (h d) -> p h d", h=4),
                        AX.X, ALU.add),
                        waits=[("act_c", a1)], inc="ve_c", tag=f"{wtag}rd{grp}_{t}")
                    a2 = a_op(p, lambda E: E.activation(rstd4[:], ssqq[:], AF.Ln,
                                                        bias=eps6[:], scale=1.0 / 128),
                              waits=[("ve_c", r1)], inc="act_c", tag=f"{wtag}ln{grp}_{t}")
                    a3 = a_op(p, lambda E: E.activation(rstd4[:], rstd4[:], AF.Exp,
                                                        bias=0.0, scale=-0.5),
                              inc="act_c", tag=f"{wtag}ex{grp}_{t}")
                    cv = v_op(p, lambda E: E.tensor_tensor(
                        qtm_sb[:].rearrange("p (h d) -> p h d", h=4),
                        qps[:, 0:512].rearrange("p (h d) -> p h d", h=4),
                        bcast_ap(rstd4, 128, 4, 128), ALU.mult),
                        waits=[("act_c", a3)], inc="ve_c", tag=f"{wtag}nm{grp}_{t}")
                for k in range(4):
                    tt = t_transpose(p, trbf[:], qtm_sb[:, k * 128:(k + 1) * 128], id_b[:],
                                     waits=[("ve_c", p.at("ve_c"))], inc="te_c",
                                     tag=f"{wtag}tt{grp}_{t}_{k}")
                    ev = v_op(p, lambda E: E.tensor_copy(qev[:], trbf[:]),
                              waits=[("te_c", tt)], inc="ve_c", tag=f"{wtag}te{grp}_{t}_{k}")
                    h = grp * 4 + k
                    p.dma("sp", out_fm_dram[h * 128:(h + 1) * 128,
                                            t * 128:(t + 1) * 128], qev[:],
                          waits=[("ve_c", ev)], stream="sp_st",
                          tag=f"{wtag}o{grp}_{t}_{k}")
                    p.ops["ve"].append((lambda E: None,
                                        [("sp_st", p.at("sp_st"))], None, 1, "qevsync"))
                prev_cv = p.at("ve_c")

    proj(wq, 4, qh_fm, 1e-6, "q")
    proj(wk, 1, kh_fm, 1e-6, "k")

    for c in range(16):
        ldv = p.dma("sp", wstage[:], wv[c * 128:(c + 1) * 128, 0:512],
                    stream="ld_v", tag=f"vld{c}")
        cv = v_op(p, lambda E, c=c: E.tensor_copy(wpan[:, c * 512:(c + 1) * 512],
                                                  wstage[:]),
                  waits=[("ld_v", ldv), ("te_c", p.at("te_c"))], inc="ve_c",
                  tag=f"vcv{c}")
        p.ops["sp"].append((lambda E: None, [("ve_c", cv)], None, 1, "wsync"))
    conv_done = p.at("ve_c")
    prev_ev = 0
    for t in range(8):
        for c in range(16):
            t_matmul(p, qps[:, 0:512],
                     xn_fm_sb[:, c * 1024 + t * 128: c * 1024 + (t + 1) * 128],
                     wpan[:, c * 512:(c + 1) * 512],
                     start=(c == 0), stop=(c == 15),
                     waits=[("ve_c", max(conv_done, prev_ev))] if c == 0 else (),
                     inc="te_c", tag=f"vmm{t}_{c}")
        te_v = p.at("te_c")
        ev = v_op(p, lambda E: E.tensor_copy(vtm_sb[:], qps[:, 0:512]),
                  waits=[("te_c", te_v), ("sp_st", p.at("sp_st"))],
                  inc="ve_c", tag=f"vev{t}")
        prev_ev = ev
        p.dma("sp", v_tm[t * 128:(t + 1) * 128, :], vtm_sb[:],
              waits=[("ve_c", ev)], stream="sp_st", tag=f"vo{t}")

    nc = p.finalize(final_waits=[("sp_st", p.at("sp_st"))])
    return nc, p

"""Launch B: L0 full (non-causal) GQA attention + Wo projection.
Inputs per core: qh_fm [2048,1024] bf16 (own tokens), kh_full [512,8192] bf16,
v_full [8192,512] bf16 (global token order), wo [2048,2048] fp32.
Output: s_tm [1024,2048] bf16.
"""

NKC = 64
def build_B():
    p = Prog("B")
    qh = p.din("qh", [2048, 1024], BF16)
    kh = p.din("kh", [512, 8192], BF16)
    vf = p.din("vf", [8192, 512], BF16)
    wo = p.din("wo", [2048, 2048], F32)
    s_tm = p.dout("s_tm", [1024, 2048], BF16)

    qh_sb = p.sb("qh_sb", [128, 16 * 1024], BF16)
    kbuf = [p.sb(f"kbuf{i}", [128, 8192], BF16) for i in range(2)]
    vbuf = [p.sb(f"vbuf{i}", [128, 64 * 128], BF16) for i in range(2)]
    es_sb = p.sb("es_sb", [128, 1024], BF16)
    ones_b = p.sb("ones_b", [128, 1], BF16)
    ones_row = p.sb("ones_row", [1, 128], F32)
    rc_sb = p.sb("rc_sb", [1, 1024], F32)
    bc_sb = p.sb("bc_sb", [128, 1024], F32)
    attn = p.sb("attn", [128, 16 * 1024], BF16)
    wstage = p.sb("wstage", [128, 512], F32)
    wpan = p.sb("wpan", [128, 16 * 512], BF16)
    stm_sb = p.sb("stm_sb", [128, 512], BF16)

    lgp = p.ps("lgp", [128, 1024], F32)
    avp = p.ps("avp", [128, 1024], F32)
    dnp = p.ps("dnp", [1, 1024], F32)
    bcp = p.ps("bcp", [128, 1024], F32)

    v_op(p, lambda E: E.memset(ones_b[:], 1.0), tag="ones_b")
    v_op(p, lambda E: E.memset(ones_row[:], 1.0), tag="ones_row")
    const_done = p.at("ve_c")

    p.dma("sp", qh_sb[:], bass.AP(qh.tensor, 0, [[1024, 128], [128 * 1024, 16], [1, 1024]]),
          stream="ldq", tag="qh")
    ldq = p.at("ldq")

    SCALE = 1.0 / (128.0 ** 0.5)
    te_after_head = [0, 0]
    for h in range(16):
        g = h // 4
        par = h % 2
        lk = p.dma("sp", kbuf[par][:], kh[g * 128:(g + 1) * 128, :],
                   waits=[("te_c", te_after_head[par])], stream=f"ldk{par}", tag=f"k{h}")
        lv = p.dma("sp", vbuf[par][:],
                   bass.AP(vf.tensor, g * 128, [[512, 128], [128 * 512, 64], [1, 128]]),
                   stream=f"ldk{par}", tag=f"v{h}")
        ldk = p.at(f"ldk{par}")
        prev_act = p.at("act_c")
        prev_ve = p.at("ve_c")
        for kc in range(NKC):
            w0 = [(f"ldk{par}", ldk), ("ldq", ldq)]
            if kc == 0:
                w0 += [("act_c", prev_act), ("ve_c", prev_ve)]
            else:
                w0 = [("act_c", p.at("act_c"))]
            t_matmul(p, lgp[:, 0:512], kbuf[par][:, kc * 128:(kc + 1) * 128],
                     qh_sb[:, h * 1024: h * 1024 + 512], True, True,
                     waits=w0, inc="te_c", tag=f"lg{h}_{kc}a")
            t_matmul(p, lgp[:, 512:1024], kbuf[par][:, kc * 128:(kc + 1) * 128],
                     qh_sb[:, h * 1024 + 512: h * 1024 + 1024], True, True,
                     inc="te_c", tag=f"lg{h}_{kc}b")
            te_lg = p.at("te_c")
            a_exp = a_op(p, lambda E: E.activation(es_sb[:], lgp[:], AF.Exp,
                                                   bias=0.0, scale=SCALE),
                         waits=[("te_c", te_lg)], tag=f"exp{h}_{kc}")
            st0 = (kc == 0)
            sp0 = (kc == NKC - 1)
            t_matmul(p, avp[:, 0:512], vbuf[par][:, kc * 128:(kc + 1) * 128],
                     es_sb[:, 0:512], st0, sp0,
                     waits=[("act_c", a_exp)], inc="te_c", tag=f"av{h}_{kc}a")
            t_matmul(p, avp[:, 512:1024], vbuf[par][:, kc * 128:(kc + 1) * 128],
                     es_sb[:, 512:1024], st0, sp0, inc="te_c", tag=f"av{h}_{kc}b")
            t_matmul(p, dnp[0:1, 0:512], ones_b[:], es_sb[:, 0:512], st0, sp0,
                     waits=[("ve_c", const_done)] if kc == 0 else (),
                     inc="te_c", tag=f"dn{h}_{kc}a")
            t_matmul(p, dnp[0:1, 512:1024], ones_b[:], es_sb[:, 512:1024], st0, sp0,
                     inc="te_c", tag=f"dn{h}_{kc}b")
        te_sweep = p.at("te_c")
        te_after_head[par] = te_sweep
        rc = v_op(p, lambda E: E.reciprocal(rc_sb[:], dnp[:]),
                  waits=[("te_c", te_sweep)], tag=f"rc{h}")
        t_matmul(p, bcp[:, 0:512], ones_row[:], rc_sb[0:1, 0:512], True, True,
                 waits=[("ve_c", rc)], inc="te_c", tag=f"bc{h}a")
        bc_mm = t_matmul(p, bcp[:, 512:1024], ones_row[:], rc_sb[0:1, 512:1024],
                         True, True, inc="te_c", tag=f"bc{h}b")
        bce = v_op(p, lambda E: E.tensor_copy(bc_sb[:], bcp[:]),
                   waits=[("te_c", bc_mm)], tag=f"bce{h}")
        v_op(p, lambda E, h=h: E.scalar_tensor_tensor(
            attn[:, h * 1024:(h + 1) * 1024], avp[:], 1.0, bc_sb[:],
            ALU.mult, ALU.mult), waits=[("ve_c", bce)], tag=f"ao{h}")

    attn_done = p.at("ve_c")

    prev_ev = 0
    for grp in range(4):
        conv_done = 0
        for c in range(16):
            ldw = p.dma("sp", wstage[:], wo[c * 128:(c + 1) * 128, grp * 512:(grp + 1) * 512],
                        stream="ldw", tag=f"wo{grp}_{c}")
            cv = v_op(p, lambda E, c=c: E.tensor_copy(wpan[:, c * 512:(c + 1) * 512],
                                                      wstage[:]),
                      waits=[("ldw", ldw), ("te_c", p.at("te_c"))], tag=f"wocv{grp}_{c}")
            conv_done = cv
            p.ops["sp"].append((lambda E: None, [("ve_c", cv)], None, 1, "wsync"))
        for t in range(8):
            for c in range(16):
                t_matmul(p, bcp[:, 0:512],
                         attn[:, c * 1024 + t * 128: c * 1024 + (t + 1) * 128],
                         wpan[:, c * 512:(c + 1) * 512],
                         start=(c == 0), stop=(c == 15),
                         waits=[("ve_c", max(conv_done, attn_done, prev_ev))] if c == 0 else (),
                         inc="te_c", tag=f"womm{grp}_{t}_{c}")
            te_w = p.at("te_c")
            ev = v_op(p, lambda E: E.tensor_copy(stm_sb[:], bcp[:, 0:512]),
                      waits=[("te_c", te_w), ("sp_st", p.at("sp_st"))], tag=f"woev{grp}_{t}")
            prev_ev = ev
            p.dma("sp", s_tm[t * 128:(t + 1) * 128, grp * 512:(grp + 1) * 512], stm_sb[:],
                  waits=[("ve_c", ev)], stream="sp_st", tag=f"so{grp}_{t}")

    nc = p.finalize(final_waits=[("sp_st", p.at("sp_st"))])
    return nc, p

"""Launch C: MoE routing + expert compute for this core's 2 experts.
Inputs: xn_full [8192,2048] bf16 (global token order), lg [8192,16] fp32,
e0vec [128,1] fp32 (=2*core), wg/wu [2,2048,1408] fp32, wd [2,1408,2048] fp32.
Output: OUT [2048,2048] bf16  (rows: local expert*1024 + slot).
"""

LISTN = 128 * 144

def build_C():
    p = Prog("C")
    xnf = p.din("xnf", [8192, 2048], BF16)
    lg = p.din("lg", [8192, 16], F32)
    e0vec = p.din("e0vec", [128, 1], F32)
    wg = p.din("wg", [2, 2048, 1408], F32)
    wu = p.din("wu", [2, 2048, 1408], F32)
    wd = p.din("wd", [2, 1408, 2048], F32)
    out = p.dout("out", [2048, 2048], BF16)

    listd = p.dtmp("listd", [144, 128], U32)

    idw = p.sb("idw", [128, 128], F32)
    id_f = p.sb("id_f", [128, 128], F32)
    id_b = p.sb("id_b", [128, 128], BF16)
    g_op(p, lambda E: E.iota(idw[:], [[-1, 128]], base=0, channel_multiplier=1,
                             allow_small_or_imprecise_dtypes=True), tag="iota")
    io = p.at("po_c")
    v_op(p, lambda E: E.tensor_scalar(id_f[:], idw[:], 0.0, None, ALU.is_equal),
         waits=[("po_c", io)], tag="idf")
    v_op(p, lambda E: E.tensor_copy(id_b[:], id_f[:]), tag="idb")
    p.id_f = id_f
    id_done = p.at("ve_c")

    e0_sb = p.sb("e0_sb", [128, 1], F32)
    p.dma("sp", e0_sb[:], e0vec, stream="ld_e0", tag="e0")

    zer = p.sb("zer", [128, 128], U32)
    ze = v_op(p, lambda E: E.memset(zer[:], 0), tag="zer")
    p.dma("sp", listd[0:128, :], zer[:], waits=[("ve_c", ze)], stream="lz", tag="lz0")
    p.dma("sp", listd[128:144, :], zer[0:16, :], waits=[("ve_c", ze)], stream="lz", tag="lz1")
    lz_done = p.at("lz")

    r = emit_routing(p, lg, own_tiles=list(range(64)))

    iota_p = p.sb("iota_p", [128, 1], F32)
    g_op(p, lambda E: E.iota(iota_p[:], [[0, 1]], base=0, channel_multiplier=1,
                             allow_small_or_imprecise_dtypes=True), tag="iop")
    iop = p.at("po_c")
    slot = p.sb("slot", [128, 1], F32)
    kept = p.sb("kept", [128, 1], F32)
    erel = p.sb("erel", [128, 1], F32)
    cge = p.sb("cge", [128, 1], F32)
    dstf = p.sb("dstf", [128, 1], F32)
    dstu = p.sb("dstu", [128, 1], U32)
    tknu = p.sb("tknu", [128, 1], U32)
    tknf = p.sb("tknf", [128, 1], F32)

    for j in range(64):
        for k in (1, 2):
            pos = (r["pos1"] if k == 1 else r["pos2"])[:, j:j + 1]
            ear = (r["earr1"] if k == 1 else r["earr2"])[:, j:j + 1]
            v_op(p, lambda E, pos=pos: E.tensor_scalar_min(slot[:], pos, 1023.0),
                 waits=[("ve_c", r["done"])] if (j == 0 and k == 1) else (), tag=f"sl{j}_{k}")
            v_op(p, lambda E, pos=pos: E.tensor_scalar(kept[:], pos, 1023.5, None,
                                                       ALU.is_le), tag=f"kp{j}_{k}")
            v_op(p, lambda E, ear=ear: E.tensor_sub(erel[:], ear, e0_sb[:]),
                 waits=[("ld_e0", p.at("ld_e0"))] if (j == 0 and k == 1) else (),
                 tag=f"er{j}_{k}")
            v_op(p, lambda E: E.tensor_scalar_add(erel[:], erel[:], 16.0), tag=f"ea{j}_{k}")
            v_op(p, lambda E: E.tensor_scalar(cge[:], erel[:], 15.5, None, ALU.is_gt),
                 tag=f"eg{j}_{k}")
            v_op(p, lambda E: E.scalar_tensor_tensor(erel[:], cge[:], -16.0, erel[:],
                                                     ALU.mult, ALU.add), tag=f"em{j}_{k}")
            v_op(p, lambda E: E.scalar_tensor_tensor(dstf[:], erel[:], 1024.0, slot[:],
                                                     ALU.mult, ALU.add), tag=f"d0{j}_{k}")
            trash_base = 16384.0 + (k - 1) * 128
            v_op(p, lambda E, tb=trash_base: E.tensor_scalar(cge[:], iota_p[:], 1.0, tb,
                                                             ALU.mult, ALU.add),
                 waits=[("po_c", iop)] if (j == 0 and k == 1) else (), tag=f"tb{j}_{k}")
            v_op(p, lambda E: E.tensor_sub(dstf[:], dstf[:], cge[:]), tag=f"d1{j}_{k}")
            v_op(p, lambda E: E.scalar_tensor_tensor(dstf[:], dstf[:], 1.0, kept[:],
                                                     ALU.mult, ALU.mult), tag=f"d2{j}_{k}")
            v_op(p, lambda E: E.tensor_add(dstf[:], dstf[:], cge[:]), tag=f"d3{j}_{k}")
            cvd = v_op(p, lambda E: E.tensor_copy(dstu[:], dstf[:]), tag=f"d4{j}_{k}")
            v_op(p, lambda E, j=j: E.tensor_scalar(tknf[:], iota_p[:], 1.0, float(j * 128),
                                                   ALU.mult, ALU.add), tag=f"tk{j}_{k}")
            cvt = v_op(p, lambda E: E.tensor_copy(tknu[:], tknf[:]), tag=f"tku{j}_{k}")
            g_op(p, lambda E: E.indirect_dma_start(
                out=bass.AP(listd.tensor, 0, [[1, 144 * 128], [1, 1]]),
                out_offset=IndirectOffsetOnAxis(dstu[:], 0),
                in_=tknu[:], in_offset=None),
                waits=[("ve_c", cvt), ("lz", lz_done)], inc="sc_dma", inc_k=16,
                tag=f"scat{j}_{k}")
            p.ops["ve"].append((lambda E: None, [("sc_dma", p.at("sc_dma"))], None, 1,
                                "scsync"))
    scat_done = p.at("sc_dma")

    lrow = p.sb("lrow", [128, 1], U32)
    xg_tm = p.sb("xg_tm", [128, 2048], BF16)
    xg_fm = p.sb("xg_fm", [128, 16 * 1024], BF16)
    trbf = p.ps("trbf", [128, 128], BF16)
    gps = p.ps("gps", [128, 512], F32)
    ups = p.ps("ups", [128, 512], F32)
    dps = p.ps("dps", [128, 512], F32)
    g_sb = p.sb("g_sb", [128, 1024], BF16)
    u_sb = p.sb("u_sb", [128, 1024], BF16)
    gu_fm = p.sb("gu_fm", [128, 11 * 1024], BF16)
    wstage = p.sb("wstage", [128, 512], F32)
    wpang = p.sb("wpang", [128, 16 * 512], BF16)
    wpanu = p.sb("wpanu", [128, 16 * 512], BF16)
    wpand = p.sb("wpand", [128, 11 * 512], BF16)
    ev_sb = p.sb("ev_sb", [128, 512], BF16)

    for le in range(2):
        for st in range(8):
            row = le * 8 + st
            ldl = p.dma("sp", lrow[:], listd[row:row + 1, :].rearrange("a b -> b a"),
                        waits=[("sc_dma", scat_done), ("gt_dma", p.counts.get("gt_dma", 0))],
                        stream="ldl", tag=f"lrow{le}_{st}")
            gth = g_op(p, lambda E: E.indirect_dma_start(
                out=xg_tm[:], out_offset=None,
                in_=xnf, in_offset=IndirectOffsetOnAxis(lrow[:], 0)),
                waits=[("ldl", ldl), ("te_c", p.at("te_c"))], inc="gt_dma", inc_k=16,
                tag=f"gth{le}_{st}")
            gt = p.at("gt_dma")
            for c in range(16):
                tt = t_transpose(p, trbf[:], xg_tm[:, c * 128:(c + 1) * 128], id_b[:],
                                 waits=[("gt_dma", gt), ("ve_c", p.at("ve_c")),
                                        ("ve_c", id_done)],
                                 inc="te_c", tag=f"xt{le}_{st}_{c}")
                v_op(p, lambda E, c=c, st=st: E.tensor_copy(
                    xg_fm[:, c * 1024 + st * 128: c * 1024 + (st + 1) * 128], trbf[:]),
                    waits=[("te_c", tt)], tag=f"xe{le}_{st}_{c}")
        xg_done = p.at("ve_c")

        for i in range(11):
            cvg = cvu = 0
            for c in range(16):
                l1 = p.dma("sp", wstage[:, 0:128],
                           wg[le, c * 128:(c + 1) * 128, i * 128:(i + 1) * 128],
                           stream="ldw", tag=f"gw{le}_{i}_{c}")
                cvg = v_op(p, lambda E, c=c: E.tensor_copy(
                    wpang[:, c * 512:c * 512 + 128], wstage[:, 0:128]),
                    waits=[("ldw", l1), ("te_c", p.at("te_c"))], tag=f"gcv{le}_{i}_{c}")
                p.ops["sp"].append((lambda E: None, [("ve_c", cvg)], None, 1, "ws"))
                l2 = p.dma("sp", wstage[:, 128:256],
                           wu[le, c * 128:(c + 1) * 128, i * 128:(i + 1) * 128],
                           stream="ldw", tag=f"uw{le}_{i}_{c}")
                cvu = v_op(p, lambda E, c=c: E.tensor_copy(
                    wpanu[:, c * 512:c * 512 + 128], wstage[:, 128:256]),
                    waits=[("ldw", l2), ("te_c", p.at("te_c"))], tag=f"ucv{le}_{i}_{c}")
                p.ops["sp"].append((lambda E: None, [("ve_c", cvu)], None, 1, "ws"))
            for sg in range(2):
                for c in range(16):
                    t_matmul(p, gps[:, 0:512], wpang[:, c * 512:c * 512 + 128],
                             xg_fm[:, c * 1024 + sg * 512: c * 1024 + (sg + 1) * 512],
                             start=(c == 0), stop=(c == 15),
                             waits=[("ve_c", max(cvu, xg_done, p.at("ve_c")))] if c == 0 else (),
                             inc="te_c", tag=f"gmm{le}_{i}_{sg}_{c}")
                te_g = p.at("te_c")
                ag = a_op(p, lambda E, sg=sg: E.activation(g_sb[:, sg * 512:(sg + 1) * 512],
                                                           gps[:, 0:512], AF.Silu),
                          waits=[("te_c", te_g)], tag=f"gsil{le}_{i}_{sg}")
                for c in range(16):
                    t_matmul(p, ups[:, 0:512], wpanu[:, c * 512:c * 512 + 128],
                             xg_fm[:, c * 1024 + sg * 512: c * 1024 + (sg + 1) * 512],
                             start=(c == 0), stop=(c == 15),
                             inc="te_c", tag=f"umm{le}_{i}_{sg}_{c}")
                te_u = p.at("te_c")
                uv = v_op(p, lambda E, sg=sg: E.tensor_copy(u_sb[:, sg * 512:(sg + 1) * 512],
                                                            ups[:, 0:512]),
                          waits=[("te_c", te_u)], tag=f"uev{le}_{i}_{sg}")
                v_op(p, lambda E, i=i, sg=sg: E.tensor_mul(
                    gu_fm[:, i * 1024 + sg * 512: i * 1024 + (sg + 1) * 512],
                    g_sb[:, sg * 512:(sg + 1) * 512], u_sb[:, sg * 512:(sg + 1) * 512]),
                    waits=[("act_c", ag)], tag=f"gu{le}_{i}_{sg}")
        gu_done = p.at("ve_c")

        for og in range(4):
            cvd = 0
            for i in range(11):
                l3 = p.dma("sp", wstage[:],
                           wd[le, i * 128:(i + 1) * 128, og * 512:(og + 1) * 512],
                           stream="ldw", tag=f"dw{le}_{og}_{i}")
                cvd = v_op(p, lambda E, i=i: E.tensor_copy(
                    wpand[:, i * 512:(i + 1) * 512], wstage[:]),
                    waits=[("ldw", l3), ("te_c", p.at("te_c"))], tag=f"dcv{le}_{og}_{i}")
                p.ops["sp"].append((lambda E: None, [("ve_c", cvd)], None, 1, "ws"))
            for st in range(8):
                for i in range(11):
                    t_matmul(p, dps[:, 0:512],
                             gu_fm[:, i * 1024 + st * 128: i * 1024 + (st + 1) * 128],
                             wpand[:, i * 512:(i + 1) * 512],
                             start=(i == 0), stop=(i == 10),
                             waits=[("ve_c", max(cvd, gu_done, p.at("ve_c")))] if i == 0 else (),
                             inc="te_c", tag=f"dmm{le}_{og}_{st}_{i}")
                te_d = p.at("te_c")
                ev = v_op(p, lambda E: E.tensor_copy(ev_sb[:], dps[:, 0:512]),
                          waits=[("te_c", te_d), ("sp_st", p.at("sp_st"))],
                          tag=f"dev{le}_{og}_{st}")
                p.dma("sp", out[le * 1024 + st * 128: le * 1024 + (st + 1) * 128,
                               og * 512:(og + 1) * 512], ev_sb[:],
                      waits=[("ve_c", ev)], stream="sp_st", tag=f"do{le}_{og}_{st}")

    nc = p.finalize(final_waits=[("sp_st", p.at("sp_st"))])
    return nc, p

"""Launch D: L0 token-side combine.
  routing -> slotids/weights for ALL tokens -> DRAM -> gather own via myidx
  y = w1*OUT[slot1] + w2*OUT[slot2]   (clamp semantics already in slotid)
  shared expert FFN (gated, IS=2816) from xn_fm
  x1 = x + s * silu(y + sh)
Inputs: outf [16384,2048] bf16, lg [8192,16] f32, xn_fm [2048,1024] bf16 (own),
s_tm [1024,2048] bf16 (own), x [1024,2048] f32 (own), myidx [1024,1] u32
(global token ids of own tokens), wsg/wsu [2048,2816] f32, wsd [2816,2048] f32.
Output: x1 [1024,2048] f32.
"""

def build_D():
    p = Prog("D")
    outf = p.din("outf", [16384, 2048], BF16)
    lg = p.din("lg", [8192, 16], F32)
    xnfm = p.din("xnfm", [2048, 1024], BF16)
    stm = p.din("stm", [1024, 2048], BF16)
    x = p.din("x", [1024, 2048], F32)
    myidx = p.din("myidx", [1024, 1], U32)
    wsg = p.din("wsg", [2048, 2816], F32)
    wsu = p.din("wsu", [2048, 2816], F32)
    wsd = p.din("wsd", [2816, 2048], F32)
    x1 = p.dout("x1", [1024, 2048], F32)

    rodram = p.dtmp("rodram", [4, 64, 128], F32)

    idw = p.sb("idw", [128, 128], F32)
    id_f = p.sb("id_f", [128, 128], F32)
    g_op(p, lambda E: E.iota(idw[:], [[-1, 128]], base=0, channel_multiplier=1,
                             allow_small_or_imprecise_dtypes=True), tag="iota")
    io = p.at("po_c")
    v_op(p, lambda E: E.tensor_scalar(id_f[:], idw[:], 0.0, None, ALU.is_equal),
         waits=[("po_c", io)], tag="idf")
    p.id_f = id_f
    id_done = p.at("ve_c")

    xnfm_sb = p.sb("xnfm_sb", [128, 16 * 1024], BF16)
    p.dma("sp", xnfm_sb[:], bass.AP(xnfm.tensor, 0, [[1024, 128], [128 * 1024, 16], [1, 1024]]),
          stream="ldx", tag="xnfm")
    ld_xn = p.at("ldx")

    with p.scope():
        r = emit_routing(p, lg, own_tiles=list(range(64)))
        sl = p.sb("sl", [128, 64], F32)
        tmp = p.sb("tmp", [128, 64], F32)
        trq = p.ps("trq", [64, 128], F32)
        sl64 = p.sb("sl64", [64, 128], F32)
        for k in (1, 2):
            posk = r["pos1"] if k == 1 else r["pos2"]
            eark = r["earr1"] if k == 1 else r["earr2"]
            v_op(p, lambda E, posk=posk: E.tensor_scalar_min(tmp[:], posk[:], 1023.0),
                 waits=[("ve_c", r["done"])], tag=f"dsl{k}")
            v_op(p, lambda E, eark=eark: E.scalar_tensor_tensor(sl[:], eark[:], 1024.0,
                                                                tmp[:], ALU.mult, ALU.add),
                 tag=f"dse{k}")
            tt = t_transpose(p, trq[:], sl[:, 0:64], id_f[:],
                             waits=[("ve_c", p.at("ve_c"))],
                             inc="te_c", tag=f"dst{k}")
            e1 = v_op(p, lambda E: E.tensor_copy(sl64[:], trq[0:64, :]),
                      waits=[("te_c", tt), ("rod", p.at("rod"))], tag=f"dse2{k}")
            p.dma("sp", bass.AP(rodram.tensor, (k - 1) * 64 * 128, [[128, 64], [1, 128]]),
                  sl64[:], waits=[("ve_c", e1)], stream="rod", tag=f"rood{k}")
            wk = r["warr1"] if k == 1 else r["warr2"]
            tw = t_transpose(p, trq[:], wk[:, 0:64], id_f[:],
                             waits=[("ve_c", p.at("ve_c"))],
                             inc="te_c", tag=f"dwt{k}")
            e2 = v_op(p, lambda E: E.tensor_copy(sl64[:], trq[0:64, :]),
                      waits=[("te_c", tw), ("rod", p.at("rod"))], tag=f"dwe2{k}")
            p.dma("sp", bass.AP(rodram.tensor, (1 + k) * 64 * 128, [[128, 64], [1, 128]]),
                  sl64[:], waits=[("ve_c", e2)], stream="rod", tag=f"wood{k}")
        rod_done = p.at("rod")
    bar_ve = p.at("ve_c"); bar_te = p.at("te_c"); bar_act = p.at("act_c")
    bar_po = p.at("po_c"); bar_rod = p.at("rod")

    myi = p.sb("myi", [128, 1], U32)
    sidf = p.sb("sidf", [128, 2], F32)
    sidu = p.sb("sidu", [128, 1], U32)
    wv = p.sb("wv", [128, 2], F32)
    gbuf = p.sb("gbuf", [128, 2048], BF16)
    y_all = p.sb("y_all", [128, 8 * 2048], BF16)
    yacc = p.sb("yacc", [128, 2048], F32)

    for t in range(8):
        lm = p.dma("sp", myi[:], myidx[t * 128:(t + 1) * 128, :],
                   waits=[("gt_dma", p.counts.get("gt_dma", 0)),
                          ("ve_c", bar_ve), ("rod", bar_rod), ("act_c", bar_act),
                          ("te_c", bar_te), ("po_c", bar_po)],
                   stream="ldm", tag=f"myi{t}")
        for k in (0, 1):
            gs = g_op(p, lambda E, k=k: E.indirect_dma_start(
                out=sidf[:, k:k + 1], out_offset=None,
                in_=bass.AP(rodram.tensor, 0, [[1, 32768], [1, 1]]),
                in_offset=IndirectOffsetOnAxis(myi[:], 0),
                element_offset=k * 64 * 128),
                waits=[("ldm", p.at("ldm")), ("ve_c", p.at("ve_c"))],
                inc="gt_dma", inc_k=16, tag=f"gsid{t}_{k}")
            gw = g_op(p, lambda E, k=k: E.indirect_dma_start(
                out=wv[:, k:k + 1], out_offset=None,
                in_=bass.AP(rodram.tensor, 0, [[1, 32768], [1, 1]]),
                in_offset=IndirectOffsetOnAxis(myi[:], 0),
                element_offset=(2 + k) * 64 * 128),
                inc="gt_dma", inc_k=16, tag=f"gw{t}_{k}")
        gt0 = p.at("gt_dma")
        for k in (0, 1):
            cu = v_op(p, lambda E, k=k: E.tensor_copy(sidu[:], sidf[:, k:k + 1]),
                      waits=[("gt_dma", gt0)], tag=f"cu{t}_{k}")
            gr = g_op(p, lambda E: E.indirect_dma_start(
                out=gbuf[:], out_offset=None, in_=outf,
                in_offset=IndirectOffsetOnAxis(sidu[:], 0)),
                waits=[("ve_c", cu), ("ve_c", p.at("ve_c"))],
                inc="gt_dma", inc_k=16, tag=f"grow{t}_{k}")
            gtk = p.at("gt_dma")
            if k == 0:
                v_op(p, lambda E, t=t: E.tensor_scalar(yacc[:], gbuf[:], wv[:, 0:1],
                                                       None, ALU.mult),
                     waits=[("gt_dma", gtk)], tag=f"y0{t}")
            else:
                v_op(p, lambda E, t=t: E.scalar_tensor_tensor(yacc[:], gbuf[:], wv[:, 1:2],
                                                              yacc[:], ALU.mult, ALU.add),
                     waits=[("gt_dma", gtk)], tag=f"y1{t}")
                v_op(p, lambda E, t=t: E.tensor_copy(y_all[:, t * 2048:(t + 1) * 2048],
                                                     yacc[:]), tag=f"yc{t}")
    y_done = p.at("ve_c")

    gu = p.sb("gu", [128, 22 * 512], BF16)
    wpg = p.sb("wpg", [128, 16 * 128], BF16)
    wpu = p.sb("wpu", [128, 16 * 128], BF16)
    wpd = p.sb("wpd", [128, 22 * 512], BF16)
    wstage = p.sb("wstage", [128, 512], F32)
    g_sb = p.sb("g_sb", [128, 512], BF16)
    u_sb = p.sb("u_sb", [128, 512], BF16)
    ys_sb = p.sb("ys_sb", [128, 512], F32)
    sil_sb = p.sb("sil_sb", [128, 512], BF16)
    s_sl = p.sb("s_sl", [128, 512], BF16)
    x_sl = p.sb("x_sl", [128, 512], F32)
    o_sb = p.sb("o_sb", [128, 512], F32)
    gps = p.ps("gps", [128, 512], F32)
    ups = p.ps("ups", [128, 512], F32)
    dps = p.ps("dps", [128, 512], F32)

    for tg in range(2):
        cols = slice(tg * 512, (tg + 1) * 512)
        for i in range(22):
            cvg = cvu = 0
            for c in range(16):
                l1 = p.dma("sp", wstage[:, 0:128],
                           wsg[c * 128:(c + 1) * 128, i * 128:(i + 1) * 128],
                           stream="ldw", tag=f"sg{tg}_{i}_{c}")
                cvg = v_op(p, lambda E, c=c: E.tensor_copy(wpg[:, c * 128:(c + 1) * 128],
                                                           wstage[:, 0:128]),
                           waits=[("ldw", l1), ("te_c", p.at("te_c"))], tag=f"sgc{tg}_{i}_{c}")
                p.ops["sp"].append((lambda E: None, [("ve_c", cvg)], None, 1, "ws"))
                l2 = p.dma("sp", wstage[:, 128:256],
                           wsu[c * 128:(c + 1) * 128, i * 128:(i + 1) * 128],
                           stream="ldw", tag=f"su{tg}_{i}_{c}")
                cvu = v_op(p, lambda E, c=c: E.tensor_copy(wpu[:, c * 128:(c + 1) * 128],
                                                           wstage[:, 128:256]),
                           waits=[("ldw", l2), ("te_c", p.at("te_c"))], tag=f"suc{tg}_{i}_{c}")
                p.ops["sp"].append((lambda E: None, [("ve_c", cvu)], None, 1, "ws"))
            for c in range(16):
                t_matmul(p, gps[:, 0:512], wpg[:, c * 128:(c + 1) * 128],
                         xnfm_sb[:, c * 1024 + tg * 512: c * 1024 + (tg + 1) * 512],
                         start=(c == 0), stop=(c == 15),
                         waits=[("ve_c", p.at("ve_c")), ("ldx", ld_xn)] if c == 0 else (),
                         inc="te_c", tag=f"sgm{tg}_{i}_{c}")
            te_g = p.at("te_c")
            ag = a_op(p, lambda E: E.activation(g_sb[:], gps[:, 0:512], AF.Silu),
                      waits=[("te_c", te_g)], tag=f"sil{tg}_{i}")
            for c in range(16):
                t_matmul(p, ups[:, 0:512], wpu[:, c * 128:(c + 1) * 128],
                         xnfm_sb[:, c * 1024 + tg * 512: c * 1024 + (tg + 1) * 512],
                         start=(c == 0), stop=(c == 15),
                         inc="te_c", tag=f"sum{tg}_{i}_{c}")
            te_u = p.at("te_c")
            uv = v_op(p, lambda E: E.tensor_copy(u_sb[:], ups[:, 0:512]),
                      waits=[("te_c", te_u)], tag=f"uev{tg}_{i}")
            v_op(p, lambda E, i=i: E.tensor_mul(gu[:, i * 512:(i + 1) * 512],
                                                g_sb[:], u_sb[:]),
                 waits=[("act_c", ag)], tag=f"gum{tg}_{i}")
        gu_done = p.at("ve_c")

        for og in range(4):
            cvd = 0
            for i in range(22):
                l3 = p.dma("sp", wstage[:],
                           wsd[i * 128:(i + 1) * 128, og * 512:(og + 1) * 512],
                           stream="ldw", tag=f"sd{tg}_{og}_{i}")
                cvd = v_op(p, lambda E, i=i: E.tensor_copy(wpd[:, i * 512:(i + 1) * 512],
                                                           wstage[:]),
                           waits=[("ldw", l3), ("te_c", p.at("te_c"))], tag=f"sdc{tg}_{og}_{i}")
                p.ops["sp"].append((lambda E: None, [("ve_c", cvd)], None, 1, "ws"))
            for tt_ in range(4):
                t = tg * 4 + tt_
                for i in range(22):
                    t_matmul(p, dps[:, 0:512],
                             gu[:, i * 512 + tt_ * 128: i * 512 + (tt_ + 1) * 128],
                             wpd[:, i * 512:(i + 1) * 512],
                             start=(i == 0), stop=(i == 21),
                             waits=[("ve_c", max(cvd, gu_done, p.at("ve_c")))] if i == 0 else (),
                             inc="te_c", tag=f"sdm{tg}_{og}_{tt_}_{i}")
                te_d = p.at("te_c")
                ls = p.dma("sp", s_sl[:], stm[t * 128:(t + 1) * 128, og * 512:(og + 1) * 512],
                           waits=[("ve_c", p.at("ve_c"))], stream="lds", tag=f"ls{t}_{og}")
                lx = p.dma("sp", x_sl[:], x[t * 128:(t + 1) * 128, og * 512:(og + 1) * 512],
                           stream="lds", tag=f"lx{t}_{og}")
                lds = p.at("lds")
                sv = v_op(p, lambda E, t=t, og=og: E.scalar_tensor_tensor(
                    ys_sb[:], dps[:, 0:512], 1.0,
                    y_all[:, t * 2048 + og * 512: t * 2048 + (og + 1) * 512],
                    ALU.mult, ALU.add),
                    waits=[("te_c", te_d), ("ve_c", y_done)], tag=f"ys{t}_{og}")
                asil = a_op(p, lambda E: E.activation(sil_sb[:], ys_sb[:], AF.Silu),
                            waits=[("ve_c", sv)], tag=f"fsil{t}_{og}")
                v_op(p, lambda E: E.tensor_mul(ys_sb[:], sil_sb[:], s_sl[:]),
                     waits=[("act_c", asil), ("lds", lds)], tag=f"sg{t}_{og}")
                ov = v_op(p, lambda E: E.tensor_add(o_sb[:], ys_sb[:], x_sl[:]),
                          waits=[("sp_st", p.counts.get("sp_st", 0))], tag=f"ox{t}_{og}")
                p.dma("sp", x1[t * 128:(t + 1) * 128, og * 512:(og + 1) * 512], o_sb[:],
                      waits=[("ve_c", ov)], stream="sp_st", tag=f"o{t}_{og}")

    nc = p.finalize(final_waits=[("sp_st", p.at("sp_st"))])
    return nc, p

"""Launch E: LN(x1) + L1 q/k/v projections with biases.
Inputs: x [1024,2048] f32 (=x1 slice), wq1 [16,2048,128] f32, bqT [128,16] f32,
wk1 [4,2048,128], bkT [128,4], wv1 [4,2048,128], bvT [128,4].
Outputs: xn_fm [2048,1024] bf16, q_fm [2048,1024] bf16, k_fm [512,1024] bf16,
v_tm [1024,512] bf16.
"""

def build_E():
    p = Prog("E")
    x = p.din("x", [1024, 2048], F32)
    wq1 = p.din("wq1", [16, 2048, 128], F32)
    bqT = p.din("bqT", [128, 16], F32)
    wk1 = p.din("wk1", [4, 2048, 128], F32)
    bkT = p.din("bkT", [128, 4], F32)
    wv1 = p.din("wv1", [4, 2048, 128], F32)
    bvT = p.din("bvT", [128, 4], F32)
    xn_fm = p.dout("xn_fm", [2048, 1024], BF16)
    q_fm = p.dout("q_fm", [2048, 1024], BF16)
    k_fm = p.dout("k_fm", [512, 1024], BF16)
    v_tm = p.dout("v_tm", [1024, 512], BF16)

    x_sb = p.sb("x_sb", [128, 8 * 2048], F32)
    xnf32 = p.sb("xnf32", [128, 2048], F32)
    xn_fm_sb = p.sb("xn_fm_sb", [128, 16 * 1024], BF16)
    sumx = p.sb("sumx", [128, 8], F32)
    ssq = p.sb("ssq", [128, 8], F32)
    meanv = p.sb("meanv", [128, 8], F32)
    varv = p.sb("varv", [128, 8], F32)
    m2 = p.sb("m2", [128, 8], F32)
    rstd = p.sb("rstd", [128, 8], F32)
    sq_scr = p.sb("sq_scr", [128, 2048], F32)
    eps5 = p.sb("eps5", [128, 1], F32)
    idw = p.sb("idw", [128, 128], F32)
    id_f = p.sb("id_f", [128, 128], F32)
    id_b = p.sb("id_b", [128, 128], BF16)
    bq_sb = p.sb("bq_sb", [128, 16], F32)
    bk_sb = p.sb("bk_sb", [128, 4], F32)
    bv_sb = p.sb("bv_sb", [128, 4], F32)
    wstage = p.sb("wstage", [128, 512], F32)
    wpan = p.sb("wpan", [128, 16 * 128], BF16)
    qe_sb = p.sb("qe_sb", [128, 1024], BF16)
    vtm_sb = p.sb("vtm_sb", [128, 128], BF16)
    trps = p.ps("trps", [128, 128], F32)
    trbf = p.ps("trbf", [128, 128], BF16)
    qps = p.ps("qps", [128, 512], F32)

    v_op(p, lambda E: E.memset(eps5[:], 1e-5), tag="eps5")
    eps_done = p.at("ve_c")
    g_op(p, lambda E: E.iota(idw[:], [[-1, 128]], base=0, channel_multiplier=1,
                             allow_small_or_imprecise_dtypes=True), tag="iota")
    io = p.at("po_c")
    v_op(p, lambda E: E.tensor_scalar(id_f[:], idw[:], 0.0, None, ALU.is_equal),
         waits=[("po_c", io)], tag="idf")
    v_op(p, lambda E: E.tensor_copy(id_b[:], id_f[:]), tag="idb")
    id_done = p.at("ve_c")

    for t in range(8):
        p.dma("sp", x_sb[:, t * 2048:(t + 1) * 2048], x[t * 128:(t + 1) * 128, :],
              stream="ld", tag=f"x{t}")
    p.dma("sp", bq_sb[:], bqT, stream="ld", tag="bq")
    p.dma("sp", bk_sb[:], bkT, stream="ld", tag="bk")
    p.dma("sp", bv_sb[:], bvT, stream="ld", tag="bv")
    ld_all = p.at("ld")

    prev_tile_te = 0
    for t in range(8):
        xt = x_sb[:, t * 2048:(t + 1) * 2048]
        v_op(p, lambda E, xt=xt, t=t: E.reduce_sum(sumx[:, t:t + 1], xt, axis=AX.X),
             waits=[("ld", ld_all)], tag=f"sum{t}")
        a_op(p, lambda E, xt=xt, t=t: E.activation(sq_scr[:], xt, AF.Square,
                                                   accum_out=ssq[:, t:t + 1]),
             waits=[("ld", ld_all)], tag=f"ssq{t}")
        act_ssq = p.at("act_c")
        v_op(p, lambda E, t=t: E.tensor_scalar(meanv[:, t:t + 1], sumx[:, t:t + 1],
                                               1.0 / 2048, None, ALU.mult), tag=f"mn{t}")
        v_op(p, lambda E, t=t: E.tensor_scalar(varv[:, t:t + 1], ssq[:, t:t + 1],
                                               1.0 / 2048, None, ALU.mult),
             waits=[("act_c", act_ssq)], tag=f"vr{t}")
        v_op(p, lambda E, t=t: E.scalar_tensor_tensor(m2[:, t:t + 1], meanv[:, t:t + 1],
                                                      1.0, meanv[:, t:t + 1],
                                                      ALU.mult, ALU.mult), tag=f"m2{t}")
        v_op(p, lambda E, t=t: E.tensor_sub(varv[:, t:t + 1], varv[:, t:t + 1],
                                            m2[:, t:t + 1]), tag=f"v2{t}")
        ve_var = p.at("ve_c")
        a_op(p, lambda E, t=t: E.activation(rstd[:, t:t + 1], varv[:, t:t + 1], AF.Ln,
                                            bias=eps5[:], scale=1.0),
             waits=[("ve_c", max(ve_var, eps_done))], tag=f"ln{t}")
        a_op(p, lambda E, t=t: E.activation(rstd[:, t:t + 1], rstd[:, t:t + 1], AF.Exp,
                                            bias=0.0, scale=-0.5), tag=f"ex{t}")
        act_rstd = p.at("act_c")
        v_op(p, lambda E, xt=xt, t=t: E.tensor_scalar(xnf32[:], xt, meanv[:, t:t + 1],
                                                      rstd[:, t:t + 1], ALU.subtract,
                                                      ALU.mult),
             waits=[("act_c", act_rstd), ("te_c", prev_tile_te)], tag=f"xn{t}")
        ve_xn = p.at("ve_c")
        for c in range(16):
            tt = t_transpose(p, trps[:], xnf32[:, c * 128:(c + 1) * 128], id_f[:],
                             waits=[("ve_c", p.at("ve_c")), ("ve_c", id_done)],
                             inc="te_c", tag=f"tr{t}_{c}")
            v_op(p, lambda E, c=c, t=t: E.tensor_copy(
                xn_fm_sb[:, c * 1024 + t * 128: c * 1024 + (t + 1) * 128], trps[:]),
                waits=[("te_c", tt)], tag=f"fme{t}_{c}")
        prev_tile_te = p.at("te_c")
    fm_done = p.at("ve_c")
    p.dma("sp", bass.AP(xn_fm.tensor, 0, [[1024, 128], [128 * 1024, 16], [1, 1024]]),
          xn_fm_sb[:], waits=[("ve_c", fm_done)], stream="sp_st", tag="oxnfm")

    def proj_fm(wdram, bias_sb, nheads, out_dram, wtag, to_tm=None):
        for h in range(nheads):
            cv = 0
            for c in range(16):
                ldw = p.dma("sp", wstage[:, 0:128],
                            wdram[h, c * 128:(c + 1) * 128, :],
                            stream=f"ld{wtag}", tag=f"{wtag}l{h}_{c}")
                cv = v_op(p, lambda E, c=c: E.tensor_copy(
                    wpan[:, c * 128:(c + 1) * 128], wstage[:, 0:128]),
                    waits=[(f"ld{wtag}", ldw), ("te_c", p.at("te_c"))],
                    tag=f"{wtag}c{h}_{c}")
                p.ops["sp"].append((lambda E: None, [("ve_c", cv)], None, 1, "ws"))
            for grp in range(2):
                for c in range(16):
                    t_matmul(p, qps[:, 0:512], wpan[:, c * 128:(c + 1) * 128],
                             xn_fm_sb[:, c * 1024 + grp * 512: c * 1024 + (grp + 1) * 512],
                             start=(c == 0), stop=(c == 15),
                             waits=[("ve_c", max(cv, fm_done)),
                                    ("act_c", p.at("act_c")),
                                    ("ve_c", p.at("ve_c"))] if c == 0 else (),
                             inc="te_c", tag=f"{wtag}m{h}_{grp}_{c}")
                te_q = p.at("te_c")
                ae = a_op(p, lambda E, h=h, grp=grp: E.activation(
                    qe_sb[:, grp * 512:(grp + 1) * 512], qps[:, 0:512], AF.Identity,
                    bias=bias_sb[:, h:h + 1], scale=1.0),
                    waits=[("te_c", te_q), ("ld", ld_all),
                           ("sp_st", p.counts.get("sp_st", 0))], tag=f"{wtag}e{h}_{grp}")
            ae_all = p.at("act_c")
            if to_tm is None:
                p.dma("sp", out_dram[h * 128:(h + 1) * 128, :], qe_sb[:],
                      waits=[("act_c", ae_all)], stream="sp_st", tag=f"{wtag}o{h}")
            else:
                for t in range(8):
                    tt = t_transpose(p, trbf[:], qe_sb[:, t * 128:(t + 1) * 128], id_b[:],
                                     waits=[("act_c", ae_all), ("ve_c", p.at("ve_c")),
                                            ("ve_c", id_done)],
                                     inc="te_c", tag=f"{wtag}tt{h}_{t}")
                    ev = v_op(p, lambda E: E.tensor_copy(vtm_sb[:], trbf[:]),
                              waits=[("te_c", tt), ("sp_st", p.counts.get("sp_st", 0))],
                              tag=f"{wtag}tv{h}_{t}")
                    p.dma("sp", out_dram[t * 128:(t + 1) * 128, h * 128:(h + 1) * 128],
                          vtm_sb[:], waits=[("ve_c", ev)], stream="sp_st",
                          tag=f"{wtag}o{h}_{t}")

    proj_fm(wq1, bq_sb, 16, q_fm, "q")
    proj_fm(wk1, bk_sb, 4, k_fm, "k")
    proj_fm(wv1, bv_sb, 4, v_tm, "v", to_tm=True)

    nc = p.finalize(final_waits=[("sp_st", p.at("sp_st"))])
    return nc, p

"""Launch F: L1 causal GQA attention (uniform-suffix scheme).
Inputs: q_fm [2048,1024] bf16, k_full [512,8192] bf16, v_full [8192,512] bf16,
masks [64,128,128] bf16 (per-core causal masks for k-chunk kc vs q-slot kc//8).
Output: s_fm [2048,1024] bf16 (normalized attention out, feature-major).
Key order is GLOBAL; q slots m cover global chunks [8m,8m+7]; for k-chunk kc the
query suffix starts at slot kc//8 (uniform across cores).
"""

def build_F():
    p = Prog("F")
    qf = p.din("qf", [2048, 1024], BF16)
    kh = p.din("kh", [512, 8192], BF16)
    vf = p.din("vf", [8192, 512], BF16)
    masks = p.din("masks", [64, 128, 128], BF16)
    s_fm = p.dout("s_fm", [2048, 1024], BF16)

    qh_sb = p.sb("qh_sb", [128, 16 * 1024], BF16)
    kbuf = [p.sb(f"kbuf{i}", [128, 8192], BF16) for i in range(2)]
    vbuf = [p.sb(f"vbuf{i}", [128, 64 * 128], BF16) for i in range(2)]
    mk_sb = p.sb("mk_sb", [128, 64 * 128], BF16)
    es_sb = p.sb("es_sb", [128, 1024], BF16)
    ones_b = p.sb("ones_b", [128, 1], BF16)
    ones_row = p.sb("ones_row", [1, 128], F32)
    rc_sb = p.sb("rc_sb", [1, 1024], F32)
    bc_sb = p.sb("bc_sb", [128, 1024], F32)
    ao_sb = p.sb("ao_sb", [128, 1024], BF16)
    lgp = p.ps("lgp", [128, 1024], F32)
    avp = p.ps("avp", [128, 1024], F32)
    dnp = p.ps("dnp", [1, 1024], F32)
    bcp = p.ps("bcp", [128, 1024], F32)

    v_op(p, lambda E: E.memset(ones_b[:], 1.0), tag="ones_b")
    v_op(p, lambda E: E.memset(ones_row[:], 1.0), tag="ones_row")
    const_done = p.at("ve_c")

    p.dma("sp", qh_sb[:], bass.AP(qf.tensor, 0, [[1024, 128], [128 * 1024, 16], [1, 1024]]),
          stream="ldq", tag="qh")
    p.dma("sp", mk_sb[:], bass.AP(masks.tensor, 0, [[128, 128], [16384, 64], [1, 128]]),
          stream="ldq", tag="masks")
    ldq = p.at("ldq")

    SCALE = 1.0 / (128.0 ** 0.5)
    te_after_head = [0, 0]
    for h in range(16):
        g = h // 4
        par = h % 2
        p.dma("sp", kbuf[par][:], kh[g * 128:(g + 1) * 128, :],
              waits=[("te_c", te_after_head[par])], stream=f"ldk{par}", tag=f"k{h}")
        p.dma("sp", vbuf[par][:],
              bass.AP(vf.tensor, g * 128, [[512, 128], [128 * 512, 64], [1, 128]]),
              stream=f"ldk{par}", tag=f"v{h}")
        ldk = p.at(f"ldk{par}")
        prev_act = p.at("act_c")
        prev_ve = p.at("ve_c")
        for kc in range(64):
            m0 = kc // 8
            s = m0 * 128
            pieces = []
            if s < 512:
                pieces.append((s, 512))
            pieces.append((max(s, 512), 1024))
            w0 = [(f"ldk{par}", ldk), ("ldq", ldq)]
            if kc == 0:
                w0 += [("act_c", prev_act), ("ve_c", prev_ve)]
            else:
                w0 = [("act_c", p.at("act_c")), ("ve_c", p.at("ve_c"))]
            first = True
            for (a, b) in pieces:
                t_matmul(p, lgp[:, a:b], kbuf[par][:, kc * 128:(kc + 1) * 128],
                         qh_sb[:, h * 1024 + a: h * 1024 + b], True, True,
                         waits=w0 if first else (), inc="te_c", tag=f"lg{h}_{kc}_{a}")
                first = False
            te_lg = p.at("te_c")
            a_exp = a_op(p, lambda E, s=s: E.activation(es_sb[:, s:1024], lgp[:, s:1024],
                                                        AF.Exp, bias=0.0, scale=SCALE),
                         waits=[("te_c", te_lg)], tag=f"exp{h}_{kc}")
            mm = v_op(p, lambda E, s=s, kc=kc: E.tensor_mul(
                es_sb[:, s:s + 128], es_sb[:, s:s + 128],
                mk_sb[:, kc * 128:(kc + 1) * 128]),
                waits=[("act_c", a_exp)], tag=f"msk{h}_{kc}")
            for (a, b) in pieces:
                st0 = (kc == 0) if a < 512 or s >= 512 else (kc == 0)
                st = (kc == 0)
                sp_ = (kc == 31) if b == 512 else (kc == 63)
                t_matmul(p, avp[:, a:b], vbuf[par][:, kc * 128:(kc + 1) * 128],
                         es_sb[:, a:b], st, sp_,
                         waits=[("ve_c", mm)], inc="te_c", tag=f"av{h}_{kc}_{a}")
                t_matmul(p, dnp[0:1, a:b], ones_b[:], es_sb[:, a:b], st, sp_,
                         waits=[("ve_c", const_done)] if kc == 0 else (),
                         inc="te_c", tag=f"dn{h}_{kc}_{a}")
        te_sweep = p.at("te_c")
        te_after_head[par] = te_sweep
        rc = v_op(p, lambda E: E.reciprocal(rc_sb[:], dnp[:]),
                  waits=[("te_c", te_sweep)], tag=f"rc{h}")
        t_matmul(p, bcp[:, 0:512], ones_row[:], rc_sb[0:1, 0:512], True, True,
                 waits=[("ve_c", rc)], inc="te_c", tag=f"bc{h}a")
        bc_mm = t_matmul(p, bcp[:, 512:1024], ones_row[:], rc_sb[0:1, 512:1024],
                         True, True, inc="te_c", tag=f"bc{h}b")
        bce = v_op(p, lambda E: E.tensor_copy(bc_sb[:], bcp[:]),
                   waits=[("te_c", bc_mm)], tag=f"bce{h}")
        ao = v_op(p, lambda E: E.scalar_tensor_tensor(ao_sb[:], avp[:], 1.0, bc_sb[:],
                                                      ALU.mult, ALU.mult),
                  waits=[("ve_c", bce), ("sp_st", p.counts.get("sp_st", 0))],
                  tag=f"ao{h}")
        p.dma("sp", s_fm[h * 128:(h + 1) * 128, :], ao_sb[:],
              waits=[("ve_c", ao)], stream="sp_st", tag=f"so{h}")

    nc = p.finalize(final_waits=[("sp_st", p.at("sp_st"))])
    return nc, p

"""Launch G: L1 ternarized FFN (relu^2) + final combine.
Inputs: xnfm [2048,1024] bf16, s1 [2048,1024] bf16, x1 [1024,2048] f32,
wfcT [2048,5120] f32 (= w_fc.T host), wprT [5120,2048] f32 (= w_proj.T host).
Output: out [1024,2048] f32 = x1 + s1 * silu(relu(xn@ternfc)^2 @ ternproj).
"""

def build_G():
    p = Prog("G")
    xnfm = p.din("xnfm", [2048, 1024], BF16)
    s1 = p.din("s1", [2048, 1024], BF16)
    x1 = p.din("x1", [1024, 2048], F32)
    wfcT = p.din("wfcT", [2048, 5120], F32)
    wprT = p.din("wprT", [5120, 2048], F32)
    out = p.dout("out", [1024, 2048], F32)

    xn_sb = p.sb("xn_sb", [128, 16 * 1024], BF16)
    p.dma("sp", xn_sb[:], bass.AP(xnfm.tensor, 0, [[1024, 128], [128 * 1024, 16], [1, 1024]]),
          stream="ldx", tag="xn")
    ld_xn = p.at("ldx")

    idw = p.sb("idw", [128, 128], F32)
    id_b = p.sb("id_b", [128, 128], BF16)
    id_f = p.sb("id_f", [128, 128], F32)
    g_op(p, lambda E: E.iota(idw[:], [[-1, 128]], base=0, channel_multiplier=1,
                             allow_small_or_imprecise_dtypes=True), tag="iota")
    io = p.at("po_c")
    v_op(p, lambda E: E.tensor_scalar(id_f[:], idw[:], 0.0, None, ALU.is_equal),
         waits=[("po_c", io)], tag="idf")
    v_op(p, lambda E: E.tensor_copy(id_b[:], id_f[:]), tag="idb")
    id_done = p.at("ve_c")

    wstage = p.sb("wstage", [128, 512], F32)
    junk = p.sb("junk", [128, 512], F32)
    csum = p.sb("csum", [128, 1], F32)
    acc = p.sb("acc", [128, 2], F32)
    onescol = p.sb("onescol", [128, 1], F32)
    ones_row = p.sb("ones_row", [1, 128], F32)
    scal = p.sb("scal", [1, 8], F32)
    thrb = p.sb("thrb", [128, 6], F32)
    tot_ps = p.ps("tot_ps", [1, 8], F32)
    bc_ps = p.ps("bc_ps", [128, 6], F32)

    v_op(p, lambda E: E.memset(onescol[:], 1.0), tag="onescol")
    v_op(p, lambda E: E.memset(ones_row[:], 1.0), tag="ones_row")
    v_op(p, lambda E: E.memset(acc[:], 0.0), tag="acc0")
    const_done = p.at("ve_c")

    av = 0
    for wi, (wd, nr, nc_, col) in enumerate([(wfcT, 2048, 5120, 0), (wprT, 5120, 2048, 1)]):
        for r in range(nr // 128):
            for cc in range(nc_ // 512):
                ldw = p.dma("sp", wstage[:], wd[r * 128:(r + 1) * 128, cc * 512:(cc + 1) * 512],
                            stream="ldw", tag=f"a{wi}_{r}_{cc}")
                aa = a_op(p, lambda E: E.activation(junk[:], wstage[:], AF.Abs,
                                                    accum_out=csum[:]),
                          waits=[("ldw", ldw), ("ve_c", av)], tag=f"ab{wi}_{r}_{cc}")
                av = v_op(p, lambda E, col=col: E.tensor_add(acc[:, col:col + 1],
                                                             acc[:, col:col + 1], csum[:]),
                          waits=[("act_c", aa), ("ve_c", const_done)], tag=f"ac{wi}_{r}_{cc}")
                p.ops["sp"].append((lambda E: None, [("act_c", aa)], None, 1, "ws"))
    acc_done = p.at("ve_c")
    tm = t_matmul(p, tot_ps[0:1, 0:2], onescol[:], acc[:, 0:2], True, True,
                  waits=[("ve_c", acc_done)], inc="te_c", tag="tot")
    sv = v_op(p, lambda E: E.tensor_copy(scal[0:1, 0:2], tot_ps[0:1, 0:2]),
              waits=[("te_c", tm)], tag="sc0")
    v_op(p, lambda E: E.tensor_scalar(scal[0:1, 0:2], scal[0:1, 0:2],
                                      1.0 / (2048.0 * 5120.0), None, ALU.mult), tag="sc1")
    v_op(p, lambda E: E.tensor_scalar_max(scal[0:1, 0:2], scal[0:1, 0:2], 1e-5), tag="sc2")
    v_op(p, lambda E: E.tensor_scalar(scal[0:1, 2:4], scal[0:1, 0:2], 0.7, None,
                                      ALU.mult), tag="sc3")
    v_op(p, lambda E: E.tensor_scalar(scal[0:1, 4:6], scal[0:1, 2:4], -1.0, None,
                                      ALU.mult), tag="sc4")
    sc_done = p.at("ve_c")
    perm = p.sb("perm", [1, 6], F32)
    v_op(p, lambda E: E.tensor_copy(perm[0:1, 0:1], scal[0:1, 2:3]), tag="p0")
    v_op(p, lambda E: E.tensor_copy(perm[0:1, 1:2], scal[0:1, 4:5]), tag="p1")
    v_op(p, lambda E: E.tensor_copy(perm[0:1, 2:3], scal[0:1, 0:1]), tag="p2")
    v_op(p, lambda E: E.tensor_copy(perm[0:1, 3:4], scal[0:1, 3:4]), tag="p3")
    v_op(p, lambda E: E.tensor_copy(perm[0:1, 4:5], scal[0:1, 5:6]), tag="p4")
    v_op(p, lambda E: E.tensor_copy(perm[0:1, 5:6], scal[0:1, 1:2]), tag="p5")
    pm_done = p.at("ve_c")
    bm = t_matmul(p, bc_ps[:, 0:6], ones_row[:], perm[0:1, 0:6], True, True,
                  waits=[("ve_c", pm_done)], inc="te_c", tag="bcast")
    tb = v_op(p, lambda E: E.tensor_copy(thrb[:], bc_ps[:, 0:6]),
              waits=[("te_c", bm)], tag="thrb")
    thr_done = tb

    c1 = p.sb("c1", [128, 512], F32)
    c2 = p.sb("c2", [128, 512], F32)

    def ternarize(dst_ap, thr_col, tag):
        v_op(p, lambda E: E.tensor_scalar(c1[:], wstage[:], thrb[:, thr_col:thr_col + 1],
                                          None, ALU.is_gt),
             waits=[("ve_c", thr_done)], tag=tag + "c1")
        v_op(p, lambda E: E.tensor_scalar(c2[:], wstage[:],
                                          thrb[:, thr_col + 1:thr_col + 2],
                                          None, ALU.is_lt), tag=tag + "c2")
        v_op(p, lambda E: E.tensor_sub(c1[:], c1[:], c2[:]), tag=tag + "d")
        return v_op(p, lambda E: E.tensor_scalar(dst_ap, c1[:],
                                                 thrb[:, thr_col + 2:thr_col + 3],
                                                 None, ALU.mult), tag=tag + "q")

    wpan4 = p.sb("wpan4", [128, 16 * 512], BF16)
    wpanp = p.sb("wpanp", [128, 40 * 128], BF16)
    h2 = p.sb("h2", [128, 40 * 512], BF16)
    r_sb = p.sb("r_sb", [128, 512], BF16)
    g_sb = p.sb("g_sb", [128, 512], BF16)
    s_sl = p.sb("s_sl", [128, 512], BF16)
    oacc = p.sb("oacc", [128, 4 * 2048], F32)
    fps = p.ps("fps", [128, 512], F32)
    pps = p.ps("pps", [128, 512], F32)
    trbf = p.ps("trbf2", [128, 128], BF16)

    for tg in range(2):
        for tsub in range(4):
            t = tg * 4 + tsub
            p.dma("sp", oacc[:, tsub * 2048:(tsub + 1) * 2048],
                  x1[t * 128:(t + 1) * 128, :],
                  waits=[("sp_st", p.counts.get("sp_st", 0)), ("ve_c", p.at("ve_c"))],
                  stream="ldx1", tag=f"x1_{t}")
        ld_x1 = p.at("ldx1")
        for hg in range(10):
            cvp = 0
            for c in range(16):
                ldw = p.dma("sp", wstage[:], wfcT[c * 128:(c + 1) * 128,
                                                  hg * 512:(hg + 1) * 512],
                            stream="ldw", tag=f"f{tg}_{hg}_{c}")
                p.ops["ve"].append((lambda E: None, [("ldw", p.at("ldw"))], None, 1, "wv"))
                cvp = ternarize(wpan4[:, c * 512:(c + 1) * 512], 0, f"tf{tg}_{hg}_{c}")
                p.ops["sp"].append((lambda E: None, [("ve_c", cvp)], None, 1, "ws"))
            for hsub in range(4):
                hc = hg * 4 + hsub
                for c in range(16):
                    t_matmul(p, fps[:, 0:512], wpan4[:, c * 512 + hsub * 128:
                                                     c * 512 + (hsub + 1) * 128],
                             xn_sb[:, c * 1024 + tg * 512: c * 1024 + (tg + 1) * 512],
                             start=(c == 0), stop=(c == 15),
                             waits=[("ve_c", max(cvp, p.at("ve_c"))), ("ldx", ld_xn)] if c == 0 else (),
                             inc="te_c", tag=f"fm{tg}_{hc}_{c}")
                te_f = p.at("te_c")
                ar = a_op(p, lambda E: E.activation(r_sb[:], fps[:, 0:512], AF.Relu),
                          waits=[("te_c", te_f)], tag=f"rl{tg}_{hc}")
                v_op(p, lambda E, hc=hc: E.tensor_mul(h2[:, hc * 512:(hc + 1) * 512],
                                                      r_sb[:], r_sb[:]),
                     waits=[("act_c", ar)], tag=f"h2{tg}_{hc}")
        h2_done = p.at("ve_c")
        for og in range(16):
            cvp = 0
            for i in range(40):
                ldw = p.dma("sp", wstage[:, 0:128], wprT[i * 128:(i + 1) * 128,
                                                        og * 128:(og + 1) * 128],
                            stream="ldw", tag=f"p{tg}_{og}_{i}")
                p.ops["ve"].append((lambda E: None, [("ldw", p.at("ldw"))], None, 1, "wv"))
                v_op(p, lambda E: E.tensor_scalar(c1[:, 0:128], wstage[:, 0:128],
                                                  thrb[:, 3:4], None, ALU.is_gt),
                     tag=f"tp1{tg}_{og}_{i}")
                v_op(p, lambda E: E.tensor_scalar(c2[:, 0:128], wstage[:, 0:128],
                                                  thrb[:, 4:5], None, ALU.is_lt),
                     tag=f"tp2{tg}_{og}_{i}")
                v_op(p, lambda E: E.tensor_sub(c1[:, 0:128], c1[:, 0:128], c2[:, 0:128]),
                     tag=f"tp3{tg}_{og}_{i}")
                cvp = v_op(p, lambda E, i=i: E.tensor_scalar(
                    wpanp[:, i * 128:(i + 1) * 128], c1[:, 0:128],
                    thrb[:, 5:6], None, ALU.mult), tag=f"tp4{tg}_{og}_{i}")
                p.ops["sp"].append((lambda E: None, [("ve_c", cvp)], None, 1, "ws"))
            for i in range(40):
                t_matmul(p, pps[:, 0:512], wpanp[:, i * 128:(i + 1) * 128],
                         h2[:, i * 512:(i + 1) * 512],
                         start=(i == 0), stop=(i == 39),
                         waits=[("ve_c", max(cvp, h2_done))] if i == 0 else (),
                         inc="te_c", tag=f"pm{tg}_{og}_{i}")
            te_p = p.at("te_c")
            asil = a_op(p, lambda E: E.activation(g_sb[:], pps[:, 0:512], AF.Silu),
                        waits=[("te_c", te_p)], tag=f"sl{tg}_{og}")
            lds = p.dma("sp", s_sl[:], bass.AP(s1.tensor, og * 128 * 1024 + tg * 512,
                                               [[1024, 128], [1, 512]]),
                        waits=[("ve_c", p.at("ve_c"))], stream="lds", tag=f"s{tg}_{og}")
            sg = v_op(p, lambda E: E.tensor_mul(g_sb[:], g_sb[:], s_sl[:]),
                      waits=[("act_c", asil), ("lds", p.at("lds"))], tag=f"sg{tg}_{og}")
            for tsub in range(4):
                tt = t_transpose(p, trbf[:], g_sb[:, tsub * 128:(tsub + 1) * 128], id_b[:],
                                 waits=[("ve_c", p.at("ve_c")), ("ve_c", id_done)],
                                 inc="te_c", tag=f"gt{tg}_{og}_{tsub}")
                v_op(p, lambda E, tsub=tsub, og=og: E.tensor_add(
                    oacc[:, tsub * 2048 + og * 128: tsub * 2048 + (og + 1) * 128],
                    oacc[:, tsub * 2048 + og * 128: tsub * 2048 + (og + 1) * 128],
                    trbf[:]),
                    waits=[("te_c", tt), ("ldx1", ld_x1)], tag=f"oa{tg}_{og}_{tsub}")
        oa_done = p.at("ve_c")
        for tsub in range(4):
            t = tg * 4 + tsub
            p.dma("sp", out[t * 128:(t + 1) * 128, :], oacc[:, tsub * 2048:(tsub + 1) * 2048],
                  waits=[("ve_c", oa_done)], stream="sp_st", tag=f"o{t}")

    nc = p.finalize(final_waits=[("sp_st", p.at("sp_st"))])
    return nc, p


T = 8192
NCORE = 8

def chunks_of(c):
    return sorted([c, 15 - c, 16 + c, 31 - c, 32 + c, 47 - c, 48 + c, 63 - c])

def tok_idx(c):
    return np.concatenate([np.arange(128 * j, 128 * (j + 1)) for j in chunks_of(c)])

def _mk_masks(c):
    ch = chunks_of(c)
    m = np.zeros((64, 128, 128), np.float32)
    kk = np.arange(128)[:, None]; qq = np.arange(128)[None, :]
    for kc in range(64):
        j = ch[kc // 8]
        if kc < j:
            m[kc] = 1.0
        elif kc == j:
            m[kc] = (128 * kc + kk <= 128 * j + qq)
    return m.astype(ml_dtypes.bfloat16)

_PROGS = {}

def _prog(name, builder):
    if name not in _PROGS:
        _PROGS[name] = builder()[0]
    return _PROGS[name]

def _run(nc, in_maps):
    res = run_bass_kernel_spmd(nc, in_maps, core_ids=list(range(NCORE)))
    return res.results

def kernel(**inputs):
    bf = ml_dtypes.bfloat16
    x_full = np.ascontiguousarray(np.asarray(inputs["L0_attn_x_q"], np.float32).reshape(T, 2048))
    tis = [tok_idx(c) for c in range(NCORE)]

    wq0 = np.ascontiguousarray(np.asarray(inputs["L0_attn_Wq"]).reshape(2048, 2048))
    wk0 = np.ascontiguousarray(np.asarray(inputs["L0_attn_Wk"]).reshape(2048, 512))
    wv0 = np.ascontiguousarray(np.asarray(inputs["L0_attn_Wv"]).reshape(2048, 512))
    gwt = np.ascontiguousarray(np.asarray(inputs["L0_ffn_gate_w"]).T)
    ncA = _prog("A", build_A)
    rA = _run(ncA, [dict(x=np.ascontiguousarray(x_full[tis[c]]), wq=wq0, wk=wk0,
                         wv=wv0, gwt=gwt) for c in range(NCORE)])

    xn_full = np.empty((T, 2048), bf)
    kh_full = np.empty((512, T), bf)
    v_full = np.empty((T, 512), bf)
    lg_full = np.empty((T, 16), np.float32)
    for c in range(NCORE):
        xn_full[tis[c]] = np.asarray(rA[c]["xn_tm"])
        kh_full[:, tis[c]] = np.asarray(rA[c]["kh_fm"])
        v_full[tis[c]] = np.asarray(rA[c]["v_tm"])
        lg_full[tis[c]] = np.asarray(rA[c]["logitsT"]).T
    xn_full = np.ascontiguousarray(xn_full)
    kh_full = np.ascontiguousarray(kh_full)
    v_full = np.ascontiguousarray(v_full)
    lg_full = np.ascontiguousarray(lg_full)

    wo0 = np.ascontiguousarray(np.asarray(inputs["L0_attn_Wo"]))
    ncB = _prog("B", build_B)
    rB = _run(ncB, [dict(qh=np.ascontiguousarray(np.asarray(rA[c]["qh_fm"])),
                         kh=kh_full, vf=v_full, wo=wo0) for c in range(NCORE)])

    wg_all = np.asarray(inputs["L0_ffn_w_gate"])
    wu_all = np.asarray(inputs["L0_ffn_w_up"])
    wd_all = np.asarray(inputs["L0_ffn_w_down"])
    ncC = _prog("C", build_C)
    rC = _run(ncC, [dict(xnf=xn_full, lg=lg_full,
                         e0vec=np.full((128, 1), 2.0 * c, np.float32),
                         wg=np.ascontiguousarray(wg_all[2 * c:2 * c + 2]),
                         wu=np.ascontiguousarray(wu_all[2 * c:2 * c + 2]),
                         wd=np.ascontiguousarray(wd_all[2 * c:2 * c + 2]))
                    for c in range(NCORE)])
    outf = np.ascontiguousarray(np.concatenate(
        [np.asarray(rC[c]["out"]) for c in range(NCORE)], axis=0))

    wsg = np.ascontiguousarray(np.asarray(inputs["L0_ffn_ws_gate"]))
    wsu = np.ascontiguousarray(np.asarray(inputs["L0_ffn_ws_up"]))
    wsd = np.ascontiguousarray(np.asarray(inputs["L0_ffn_ws_down"]))
    ncD = _prog("D", build_D)
    rD = _run(ncD, [dict(outf=outf, lg=lg_full,
                         xnfm=np.ascontiguousarray(np.asarray(rA[c]["xn_fm"])),
                         stm=np.ascontiguousarray(np.asarray(rB[c]["s_tm"])),
                         x=np.ascontiguousarray(x_full[tis[c]]),
                         myidx=tis[c].astype(np.uint32).reshape(1024, 1),
                         wsg=wsg, wsu=wsu, wsd=wsd) for c in range(NCORE)])

    wq1 = np.ascontiguousarray(np.asarray(inputs["L1_attn_Wq"]))
    wk1 = np.ascontiguousarray(np.asarray(inputs["L1_attn_Wk"]))
    wv1 = np.ascontiguousarray(np.asarray(inputs["L1_attn_Wv"]))
    bqT = np.ascontiguousarray(np.asarray(inputs["L1_attn_bq"]).T)
    bkT = np.ascontiguousarray(np.asarray(inputs["L1_attn_bk"]).T)
    bvT = np.ascontiguousarray(np.asarray(inputs["L1_attn_bv"]).T)
    ncE = _prog("E", build_E)
    rE = _run(ncE, [dict(x=np.ascontiguousarray(np.asarray(rD[c]["x1"])),
                         wq1=wq1, bqT=bqT, wk1=wk1, bkT=bkT, wv1=wv1, bvT=bvT)
                    for c in range(NCORE)])

    k1_full = np.empty((512, T), bf)
    v1_full = np.empty((T, 512), bf)
    for c in range(NCORE):
        k1_full[:, tis[c]] = np.asarray(rE[c]["k_fm"])
        v1_full[tis[c]] = np.asarray(rE[c]["v_tm"])
    k1_full = np.ascontiguousarray(k1_full)
    v1_full = np.ascontiguousarray(v1_full)

    ncF = _prog("F", build_F)
    rF = _run(ncF, [dict(qf=np.ascontiguousarray(np.asarray(rE[c]["q_fm"])),
                         kh=k1_full, vf=v1_full, masks=_mk_masks(c))
                    for c in range(NCORE)])

    wfcT = np.ascontiguousarray(np.asarray(inputs["L1_ffn_w_fc"]).T)
    wprT = np.ascontiguousarray(np.asarray(inputs["L1_ffn_w_proj"]).T)
    ncG = _prog("G", build_G)
    rG = _run(ncG, [dict(xnfm=np.ascontiguousarray(np.asarray(rE[c]["xn_fm"])),
                         s1=np.ascontiguousarray(np.asarray(rF[c]["s_fm"])),
                         x1=np.ascontiguousarray(np.asarray(rD[c]["x1"])),
                         wfcT=wfcT, wprT=wprT) for c in range(NCORE)])

    final = np.empty((T, 2048), np.float32)
    for c in range(NCORE):
        final[tis[c]] = np.asarray(rG[c]["out"])
    return final.reshape(1, T, 2048)
